# version 03
# baseline (speedup 1.0000x reference)
import numpy as np
import ml_dtypes
from contextlib import ExitStack
import concourse.bass as bass
import concourse.mybir as mybir
from concourse.bass_utils import run_bass_kernel_spmd

F32 = mybir.dt.float32
BF16 = mybir.dt.bfloat16
F16 = mybir.dt.float16
I32 = mybir.dt.int32
FP8 = mybir.dt.float8e4
AF = mybir.ActivationFunctionType
ALU = mybir.AluOpType
AX = mybir.AxisListType

NCORES = 8
T = 2048
D = 2048
NT = T // 128
ND = D // 128
BF = np.dtype(ml_dtypes.bfloat16)

MAGIC_RNE = 12582912.0


class Op:
    __slots__ = ("engine", "fn", "reads", "writes", "is_dma", "is_tp", "waits", "tok", "tag")

    def __init__(self, engine, fn, reads, writes, is_dma=False, is_tp=False, tag=""):
        self.engine = engine
        self.fn = fn
        self.reads = reads
        self.writes = writes
        self.is_dma = is_dma
        self.is_tp = is_tp
        self.waits = []
        self.tok = None
        self.tag = tag


def _names(aps):
    out = []
    for ap in aps:
        if ap is None:
            continue
        t = ap.tensor if isinstance(ap, bass.AP) else ap
        out.append(t.name)
    return out


class B:
    ENGINES = ("tensor", "vector", "scalar", "gpsimd", "sync")

    def __init__(self, nc):
        self.nc = nc
        self.ctx = ExitStack()
        self.ops = []
        self.streams = {e: [] for e in self.ENGINES}
        self.dma_count = {e: 0 for e in self.ENGINES}
        self.psum_names = set()

    def sb(self, name, shape, dtype):
        return self.ctx.enter_context(self.nc.sbuf_tensor(name, list(shape), dtype))

    def ps(self, name, shape, dtype=F32):
        self.psum_names.add(name)
        return self.ctx.enter_context(self.nc.psum_tensor(name, list(shape), dtype))

    def op(self, engine, fn, reads, writes, tag=""):
        o = Op(engine, fn, _names(reads), _names(writes), tag=tag)
        idx = len(self.streams[engine])
        o.tok = ("e", engine, idx)
        self.streams[engine].append(o)
        self.ops.append(o)
        return o

    def dma(self, queue, out, in_, tag="", **kw):
        assert queue in ("sync", "gpsimd", "scalar")
        self.dma_count[queue] += 1
        k = self.dma_count[queue]
        o = Op(queue, lambda eng, out=out, in_=in_, kw=kw: eng.dma_start(out=out, in_=in_, **kw),
               _names([in_]), _names([out]), is_dma=True, tag=tag)
        o.tok = ("d", queue, k)
        self.streams[queue].append(o)
        self.ops.append(o)
        return o

    def dma_tp(self, out, in_, tag=""):
        queue = "sync"
        self.dma_count[queue] += 1
        k = self.dma_count[queue]
        o = Op(queue, lambda eng, out=out, in_=in_: eng.dma_start_transpose(out, in_),
               _names([in_]), _names([out]), is_dma=True, is_tp=True, tag=tag)
        o.tok = ("d", queue, k)
        self.streams[queue].append(o)
        self.ops.append(o)
        return o

    def v(self, fname, out, *ins, tag="", **kw):
        aps = [a for a in ins if isinstance(a, bass.AP)]
        return self.op("vector", lambda eng, out=out, ins=ins, kw=kw: getattr(eng, fname)(out, *ins, **kw),
                       aps, [out], tag=tag)

    def s(self, fname, out, *ins, tag="", **kw):
        aps = [a for a in ins if isinstance(a, bass.AP)]
        extra_w = [kw["accum_out"]] if kw.get("accum_out") is not None else []
        return self.op("scalar", lambda eng, out=out, ins=ins, kw=kw: getattr(eng, fname)(out, *ins, **kw),
                       aps, [out] + extra_w, tag=tag)

    def g(self, fname, out, *ins, tag="", **kw):
        aps = [a for a in ins if isinstance(a, bass.AP)]
        return self.op("gpsimd", lambda eng, out=out, ins=ins, kw=kw: getattr(eng, fname)(out, *ins, **kw),
                       aps, [out], tag=tag)

    def mm(self, out, lhsT, rhs, start, stop, tag=""):
        return self.op("tensor",
                       lambda eng, out=out, l=lhsT, r=rhs, a=start, b=stop: eng.matmul(
                           out, l, r, start=a, stop=b, skip_group_check=True),
                       [lhsT, rhs], [out], tag=tag)

    def finalize(self, paranoid=False):
        state = {}
        outstanding = {q: 0 for q in ("sync", "gpsimd", "scalar")}
        last_tp = None

        def skey(tok):
            return (tok[0], tok[1])

        def want(op, tok):
            if tok is None:
                return
            if tok[0] == "e" and tok[1] == op.engine == "tensor":
                return
            if tok == op.tok:
                return
            op.waits.append(tok)

        DPOOL = 8

        prev = None
        for op in self.ops:
            if paranoid and prev is not None:
                want(op, prev)
            prev = op.tok
            if op.is_dma:
                k = op.tok[2]
                if k > DPOOL:
                    want(op, ("d", op.engine, k - DPOOL))
            if op.is_tp:
                for q, cnt in outstanding.items():
                    for j in range(max(1, cnt - DPOOL + 1), cnt + 1):
                        want(op, ("d", q, j))
            elif op.is_dma and last_tp is not None:
                want(op, last_tp)
            for nm in op.reads:
                st = state.get(nm)
                if st:
                    for t in st["w"].values():
                        want(op, t)
                    if nm in self.psum_names:
                        for t in st["r"].values():
                            want(op, t)
            for nm in op.writes:
                st = state.get(nm)
                if st:
                    for t in st["w"].values():
                        want(op, t)
                    for t in st["r"].values():
                        want(op, t)
            for nm in op.reads:
                state.setdefault(nm, {"w": {}, "r": {}})["r"][skey(op.tok)] = op.tok
            for nm in op.writes:
                st = state.setdefault(nm, {"w": {}, "r": {}})
                st["w"] = {skey(op.tok): op.tok}
                st["r"] = {}
            if op.is_dma:
                outstanding[op.engine] = op.tok[2]
                if op.is_tp:
                    last_tp = op.tok

        need_inc = {e: set() for e in self.ENGINES}
        for op in self.ops:
            for t in op.waits:
                if t[0] == "e":
                    need_inc[t[1]].add(t[2])
        inc_rank = {}
        for e in self.ENGINES:
            for rank, idx in enumerate(sorted(need_inc[e])):
                inc_rank[(e, idx)] = rank + 1

        nc = self.nc
        esem = {e: self.ctx.enter_context(nc.semaphore(f"sem_{e}")) for e in self.ENGINES}
        dsem = {q: [self.ctx.enter_context(nc.semaphore(f"dsem_{q}{i}")) for i in range(DPOOL)]
                for q in ("sync", "gpsimd", "scalar")}

        def dsem_for(q, k):
            return dsem[q][(k - 1) % DPOOL], 16 * ((k - 1) // DPOOL + 1)

        def target(tok):
            if tok[0] == "e":
                return esem[tok[1]], inc_rank[(tok[1], tok[2])]
            return dsem_for(tok[1], tok[2])

        streams = self.streams
        final_dma = {q: self.dma_count[q] for q in ("sync", "gpsimd", "scalar")}

        with nc.Block() as block:
            def run_stream(eng_obj, ename):
                waited = {}
                for idx, op in enumerate(streams[ename]):
                    for t in op.waits:
                        sem, val = target(t)
                        if waited.get(sem.name, -1) >= val:
                            continue
                        eng_obj.wait_ge(sem, val)
                        waited[sem.name] = val
                    inst = op.fn(eng_obj)
                    if op.is_dma:
                        s_, _v = dsem_for(ename, op.tok[2])
                        inst.then_inc(s_, 16)
                    elif (ename, idx) in inc_rank:
                        inst.then_inc(esem[ename], 1)
                if ename in final_dma and final_dma[ename] > 0:
                    cnt = final_dma[ename]
                    for j in range(max(1, cnt - DPOOL + 1), cnt + 1):
                        sem, val = dsem_for(ename, j)
                        eng_obj.wait_ge(sem, val)

            block.tensor(lambda eng: run_stream(eng, "tensor"))
            block.vector(lambda eng: run_stream(eng, "vector"))
            block.scalar(lambda eng: run_stream(eng, "scalar"))
            block.gpsimd(lambda eng: run_stream(eng, "gpsimd"))
            block.sync(lambda eng: run_stream(eng, "sync"))


def make_nc():
    return bass.Bass("TRN2", target_bir_lowering=False, debug=False)



def alloc_common(b, nc, n_big=2, skip=()):
    c = {}
    for i, nm in enumerate(["BIGA", "BIGB", "BIGC"][:n_big]):
        c[nm] = b.sb(nm, [128, ND * T], BF16)
    c["WEC"] = b.sb("WEC", [128, ND * 512], BF16)
    if "ATT" not in skip:
        c["ATT"] = b.sb("ATT", [128, 8 * 512], BF16)
    if "VH" not in skip:
        c["VH"] = b.sb("VH", [128, 16 * 128], BF16)
    if "KH" not in skip:
        c["KH"] = b.sb("KH", [128, T], BF16)
    if "QCH" not in skip:
        c["QCH"] = b.sb("QCH", [128, 2 * 512], BF16)
    c["SCR0"] = b.sb("SCR0", [128, 2048], F32)
    c["SCR1"] = b.sb("SCR1", [128, 2048], F32)
    c["TA"] = b.sb("TA", [128, 2048], BF16)
    c["TB"] = b.sb("TB", [128, 2048], BF16)
    c["TC"] = b.sb("TC", [128, 2048], BF16)
    c["TD"] = b.sb("TD", [128, 2048], BF16)
    c["MCOL"] = b.sb("MCOL", [128, NT], F32)
    c["RCOL"] = b.sb("RCOL", [128, NT], F32)
    c["SM0"] = b.sb("SM0", [128, 64], F32)
    c["SM1"] = b.sb("SM1", [128, 64], F32)
    c["SM2"] = b.sb("SM2", [128, 64], F32)
    c["DSTG"] = b.sb("DSTG", [1, 512], F32)
    c["DSTGB"] = b.sb("DSTGB", [1, 512], BF16)
    c["RSB"] = b.sb("RSB", [128, 512], BF16)
    if "BROW" not in skip:
        c["BROW"] = b.sb("BROW", [1, 512], BF16)
    ones_row = nc.inline_tensor(np.ones((1, 128), dtype=BF), "c_ones_row")
    c["ONESROW"] = b.sb("ONESROW", [1, 128], BF16)
    b.dma("sync", c["ONESROW"][:], ones_row.ap(), tag="c0")
    ones_col = nc.inline_tensor(np.ones((128, 1), dtype=BF), "c_ones_col")
    c["ONESCOL"] = b.sb("ONESCOL", [128, 1], BF16)
    b.dma("sync", c["ONESCOL"][:], ones_col.ap(), tag="c1")
    masks = np.zeros((4, 128, 512), dtype=BF)
    p = np.arange(128)[:, None]
    f = np.arange(512)[None, :]
    for j in range(4):
        masks[j] = (p <= f - 128 * j).astype(BF)
    mk = nc.inline_tensor(masks.reshape(4 * 128, 512), "c_masks")
    c["MASKS"] = b.sb("MASKS", [128, 4 * 512], BF16)
    for j in range(4):
        b.dma("sync", c["MASKS"][:, j * 512:(j + 1) * 512], mk.ap()[j * 128:(j + 1) * 128, :], tag="cm")
    c["PS"] = [b.ps(f"ps{i}", [128, 512]) for i in range(6)]
    c["PSR"] = b.ps("psr", [128, 512])
    c["PSS"] = b.ps("pss", [128, 512])
    return c


def big3(t):
    return t.ap().rearrange("p (m t) -> p m t", m=ND)


def prologue_ln(b, c, x_in, XT, extra_tile_fn=None, save_var=None):
    xnbuf = [c["TA"], c["TA"]]
    xbuf = [c["SCR0"], c["SCR1"]]
    for tt in range(NT):
        xb = xbuf[tt % 2]
        xn = xnbuf[tt % 2]
        b.dma("sync", xb[:], x_in[tt * 128:(tt + 1) * 128, :], tag=f"ldx{tt}")
        b.v("reduce_sum", c["SM0"][:, tt:tt + 1], xb[:], axis=AX.X, tag="m")
        b.v("tensor_scalar_mul", c["MCOL"][:, tt:tt + 1], c["SM0"][:, tt:tt + 1], 1.0 / D, tag="m")
        b.s("activation", c["TC"][:], xb[:], AF.Square, accum_out=c["SM1"][:, tt:tt + 1], tag="sq")
        b.v("tensor_mul", c["SM2"][:, tt:tt + 1], c["MCOL"][:, tt:tt + 1], c["MCOL"][:, tt:tt + 1], tag="v")
        b.v("scalar_tensor_tensor", c["SM2"][:, tt:tt + 1], c["SM1"][:, tt:tt + 1], 1.0 / D, c["SM2"][:, tt:tt + 1], ALU.mult, ALU.subtract, tag="v")
        if save_var is not None:
            b.v("tensor_copy", save_var[:, tt:tt + 1], c["SM2"][:, tt:tt + 1], tag="vv")
        b.v("tensor_scalar_add", c["SM2"][:, tt:tt + 1], c["SM2"][:, tt:tt + 1], 1e-5, tag="v")
        b.s("activation", c["SM2"][:, tt:tt + 1], c["SM2"][:, tt:tt + 1], AF.Ln, tag="v")
        b.s("activation", c["RCOL"][:, tt:tt + 1], c["SM2"][:, tt:tt + 1], AF.Exp, scale=-0.5, tag="v")
        b.v("tensor_scalar", xn[:], xb[:], c["MCOL"][:, tt:tt + 1], c["RCOL"][:, tt:tt + 1], ALU.subtract, ALU.mult, tag="xn")
        b.dma_tp(big3(XT)[:, :, tt * 128:(tt + 1) * 128], xn[:], tag=f"tpx{tt}")
        if extra_tile_fn is not None:
            extra_tile_fn(tt, xn, xb)


def stream_matmul_TxF(b, c, XT, wT_dram, out_cb, n_out, psum_pool, bias_row=None, tag="mm"):
    wec3 = c["WEC"].ap().rearrange("p (m n) -> p m n", m=ND)
    for ec in range(n_out // 512):
        for dt in range(ND):
            b.dma("gpsimd", wec3[:, dt, :],
                  wT_dram[dt * 128:(dt + 1) * 128, ec * 512:(ec + 1) * 512], tag=f"{tag}w{ec}_{dt}")
        for tt in range(NT):
            ps = psum_pool[tt % len(psum_pool)]
            for dt in range(ND):
                b.mm(ps[:], big3(XT)[:, dt, tt * 128:(tt + 1) * 128], wec3[:, dt, :],
                     start=(dt == 0), stop=(dt == ND - 1 and bias_row is None), tag=tag)
            if bias_row is not None:
                b.mm(ps[:], c["ONESROW"][0:1, :], bias_row(ec), start=False, stop=True, tag=f"{tag}b")
            out_cb(tt, ec, ps)


def epilogue(b, c, x_in, x_out, s_src_fn, g_src_fn, add_xn=False):
    for tt in range(NT):
        xb = c["SCR0"] if tt % 2 == 0 else c["SCR1"]
        b.dma("scalar", xb[:], x_in[tt * 128:(tt + 1) * 128, :], tag=f"ex{tt}")
        g_ap = g_src_fn(tt)
        s_ap = s_src_fn(tt)
        if add_xn:
            b.v("tensor_scalar", c["TC"][:], xb[:], c["MCOL"][:, tt:tt + 1], c["RCOL"][:, tt:tt + 1], ALU.subtract, ALU.mult, tag="exn")
            b.v("tensor_add", c["TC"][:], c["TC"][:], s_ap, tag="es")
            s_eff = c["TC"][:]
        else:
            s_eff = s_ap
        b.v("tensor_tensor", c["TD"][:], s_eff, g_ap, ALU.mult, tag="esg")
        b.v("tensor_add", xb[:], xb[:], c["TD"][:], tag="exx")
        b.dma("gpsimd", x_out[tt * 128:(tt + 1) * 128, :], xb[:], tag=f"sx{tt}")


def fused_norm_evac(b, c, po, pd, dst_ap, eps, tag):
    b.v("tensor_copy", c["DSTG"][:], pd[0:1, :], tag=tag)
    if eps:
        b.v("tensor_scalar_add", c["DSTG"][:], c["DSTG"][:], eps, tag=tag)
    b.v("reciprocal", c["DSTG"][:], c["DSTG"][:], tag=tag)
    b.v("tensor_copy", c["DSTGB"][:], c["DSTG"][:], tag=tag)
    b.mm(c["PSR"][:], c["ONESROW"][0:1, :], c["DSTGB"][:], start=True, stop=True, tag=tag)
    b.v("tensor_copy", c["RSB"][:], c["PSR"][:], tag=tag)
    b.v("tensor_tensor", dst_ap, po[:], c["RSB"][:], ALU.mult, tag=tag)



def build_layer1(phases=8, dbg=False):
    nc = make_nc()
    x_in = nc.dram_tensor("x_in", [T, D], F32, kind="ExternalInput").ap()
    WqT = nc.dram_tensor("WqT", [D, D], BF16, kind="ExternalInput").ap()
    WkT = nc.dram_tensor("WkT", [D, D], BF16, kind="ExternalInput").ap()
    WvT = nc.dram_tensor("WvT", [D, D], BF16, kind="ExternalInput").ap()
    WoT = nc.dram_tensor("WoT", [D, D], BF16, kind="ExternalInput").ap()
    beta_in = nc.dram_tensor("beta", [1, 128], F32, kind="ExternalInput").ap()
    q2_in = nc.dram_tensor("q2", [D, D], FP8, kind="ExternalInput").ap()
    x_out = nc.dram_tensor("x_out", [T, D], F32, kind="ExternalOutput").ap()

    spill_kind = "ExternalOutput" if dbg else "Internal"
    V_sp = nc.dram_tensor("V_sp", [T, D], BF16, kind=spill_kind).ap()
    G_sp = nc.dram_tensor("G_sp", [T, D], BF16, kind=spill_kind).ap()
    KT_sp = nc.dram_tensor("KT_sp", [D, T], BF16, kind=spill_kind).ap()
    if dbg:
        dbg_xt = nc.dram_tensor("dbg_xt", [D, T], BF16, kind="ExternalOutput").ap()
        dbg_qft = nc.dram_tensor("dbg_qft", [D, T], BF16, kind="ExternalOutput").ap()
        dbg_ot = nc.dram_tensor("dbg_ot", [D, T], BF16, kind="ExternalOutput").ap()
        dbg_s = nc.dram_tensor("dbg_s", [T, D], BF16, kind="ExternalOutput").ap()

    b = B(nc)
    c = alloc_common(b, nc, n_big=2)
    XT = c["BIGA"]
    PS = c["PS"]

    BR0 = b.sb("BR0", [1, 128], F32)
    b.dma("sync", BR0[:], beta_in, tag="lbeta")
    brow_src = bass.AP(BR0.ap().tensor, BR0.ap().offset, [BR0.ap().ap[0], [0, 4], [1, 128]])
    b.v("tensor_copy", c["BROW"].ap().rearrange("p (m f) -> p m f", m=4), brow_src, tag="beta")

    prologue_ln(b, c, x_in, XT)

    if dbg:
        for dt in range(ND):
            b.dma("gpsimd", dbg_xt[dt * 128:(dt + 1) * 128, :], big3(XT)[:, dt, :], tag="dbgxt")

    def v_out(tt, ec, ps):
        b.v("tensor_copy", c["TC"][:, 0:512], ps[:], tag="vev")
        b.dma("gpsimd", V_sp[tt * 128:(tt + 1) * 128, ec * 512:(ec + 1) * 512], c["TC"][:, 0:512], tag="vsp")
    if phases >= 2:
        stream_matmul_TxF(b, c, XT, WvT, v_out, D, PS[0:2], tag="V")

    if dbg:
        dbg_ps = nc.dram_tensor("dbg_ps", [128, 512], F32, kind="ExternalOutput").ap()
        dbg_tc = nc.dram_tensor("dbg_tc", [128, 512], F32, kind="ExternalOutput").ap()
        dbg_sm = nc.dram_tensor("dbg_sm", [128, 16], F32, kind="ExternalOutput").ap()
        DBGM = b.sb("DBGM", [128, 16], F32)
        b.v("memset", DBGM[:], 0.0, tag="snap")

    def qk_proj(wT_dram, dst3, spill, scale_out, tag):
        wec3 = c["WEC"].ap().rearrange("p (m n) -> p m n", m=ND)
        for ec in range(4):
            for dt in range(ND):
                b.dma("gpsimd", wec3[:, dt, :], wT_dram[dt * 128:(dt + 1) * 128, ec * 512:(ec + 1) * 512], tag=f"{tag}w{ec}")
            for tt in range(NT):
                ps = PS[4 + tt % 2]
                for dt in range(ND):
                    b.mm(ps[:], big3(XT)[:, dt, tt * 128:(tt + 1) * 128], wec3[:, dt, :],
                         start=(dt == 0), stop=False, tag=tag)
                b.mm(ps[:], c["ONESROW"][0:1, :], c["BROW"][0:1, :], start=False, stop=True, tag=f"{tag}b")
                co = (4 * tt) % 64
                snap = dbg and tag == "q" and ec == 3 and tt == 15
                ps3 = ps.ap().rearrange("p (h e) -> p h e", h=4)
                if snap:
                    b.v("tensor_copy", c["SCR1"][:, 512:1024], ps[:], tag="snap")
                    b.dma("gpsimd", dbg_ps, c["SCR1"][:, 512:1024], tag="snap")
                b.v("reduce_sum", c["SM0"][:, co:co + 4], ps3, axis=AX.X, tag=f"{tag}st")
                b.s("activation", c["TC"][:, 0:512], ps[:], AF.Square, tag=f"{tag}sq")
                b.v("reduce_sum", c["SM1"][:, co:co + 4], c["TC"][:, 0:512].rearrange("p (h e) -> p h e", h=4), axis=AX.X, tag=f"{tag}st")
                b.v("tensor_scalar_mul", c["SM0"][:, co:co + 4], c["SM0"][:, co:co + 4], 1.0 / 128, tag=f"{tag}st")
                b.v("tensor_mul", c["SM2"][:, co:co + 4], c["SM0"][:, co:co + 4], c["SM0"][:, co:co + 4], tag=f"{tag}st")
                b.v("scalar_tensor_tensor", c["SM2"][:, co:co + 4], c["SM1"][:, co:co + 4], 1.0 / 128, c["SM2"][:, co:co + 4], ALU.mult, ALU.subtract, tag=f"{tag}st")
                b.v("tensor_scalar_add", c["SM2"][:, co:co + 4], c["SM2"][:, co:co + 4], 1e-5, tag=f"{tag}st")
                b.s("activation", c["SM2"][:, co:co + 4], c["SM2"][:, co:co + 4], AF.Ln, tag=f"{tag}st")
                b.s("activation", c["SM2"][:, co:co + 4], c["SM2"][:, co:co + 4], AF.Exp, scale=-0.5, tag=f"{tag}st")
                if scale_out != 1.0:
                    b.v("tensor_scalar_mul", c["SM2"][:, co:co + 4], c["SM2"][:, co:co + 4], scale_out, tag=f"{tag}st")
                b.v("tensor_mul", c["SM0"][:, co:co + 4], c["SM0"][:, co:co + 4], c["SM2"][:, co:co + 4], tag=f"{tag}st")
                if snap:
                    b.v("tensor_copy", DBGM[:, 8:12], c["SM2"][:, co:co + 4], tag="snap")
                    b.v("tensor_copy", DBGM[:, 12:16], c["SM0"][:, co:co + 4], tag="snap")
                    b.dma("gpsimd", dbg_sm, DBGM[:], tag="snap")
                a_src = bass.AP(c["SM2"].ap().tensor, c["SM2"].ap().offset + co, [c["SM2"].ap().ap[0], [1, 4], [0, 128]])
                m_src = bass.AP(c["SM0"].ap().tensor, c["SM0"].ap().offset + co, [c["SM0"].ap().ap[0], [1, 4], [0, 128]])
                b.v("tensor_copy", c["SCR0"][:, 0:512].rearrange("p (h e) -> p h e", h=4), a_src, tag=f"{tag}ex")
                b.v("tensor_copy", c["SCR1"][:, 0:512].rearrange("p (h e) -> p h e", h=4), m_src, tag=f"{tag}ex")
                qt = c["QCH"][:, (tt % 2) * 512:(tt % 2) * 512 + 512]
                b.v("tensor_tensor", c["TC"][:, 0:512], ps[:], c["SCR0"][:, 0:512], ALU.mult, tag=f"{tag}ap")
                b.v("tensor_sub", qt, c["TC"][:, 0:512], c["SCR1"][:, 0:512], tag=f"{tag}ap")
                if spill is None:
                    b.dma_tp(dst3[:, 4 * ec:4 * ec + 4, tt * 128:(tt + 1) * 128], qt, tag=f"{tag}tp")
                else:
                    stg = c["TD"].ap().rearrange("p (m f) -> p m f", m=ND)[:, 0:4, 0:128]
                    b.dma_tp(stg, qt, tag=f"{tag}tp")
                    dram = spill.rearrange("(g m p) t -> p m g t", p=128, m=4)[:, :, ec, tt * 128:(tt + 1) * 128]
                    b.dma("gpsimd", dram, c["TD"].ap().rearrange("p (m f) -> p m f", m=ND)[:, 0:4, 0:128], tag=f"{tag}sp")

    if phases >= 3:
        qk_proj(WqT, big3(c["BIGB"]), None, 128.0 ** -0.5, "q")
    if phases >= 4:
        qk_proj(WkT, None, KT_sp, 1.0, "k")
    QFT = c["BIGB"]

    if dbg:
        for dt in range(ND):
            b.dma("gpsimd", dbg_qft[dt * 128:(dt + 1) * 128, :], big3(c["BIGB"])[:, dt, :], tag="dbgq")

    WECF8 = c["WEC"].bitcast(FP8)
    wecf3 = WECF8.ap()[:, 0:ND * 512].rearrange("p (m n) -> p m n", m=ND)
    Q1C = b.sb("Q1C", [128, ND * 128], FP8)
    q1c3 = Q1C.ap().rearrange("p (m t) -> p m t", m=ND)
    for ec in range(4 if phases >= 5 else 0):
        for dt in range(ND):
            b.dma("gpsimd", wecf3[:, dt, :], q2_in[dt * 128:(dt + 1) * 128, ec * 512:(ec + 1) * 512], tag=f"q2w{ec}")
        for tt in range(NT):
            if ec == 0:
                pass
            b.v("tensor_scalar_mul", q1c3[:, :, :], big3(XT)[:, :, tt * 128:(tt + 1) * 128], 0.5, tag="q1")
            ps = PS[2 + tt % 2]
            for dt in range(ND):
                b.mm(ps[:], q1c3[:, dt, :], wecf3[:, dt, :], start=(dt == 0), stop=(dt == ND - 1), tag="ffn")
            b.s("activation", c["TC"][:, 0:512], ps[:], AF.Silu, scale=4.0, tag="gsilu")
            b.dma("gpsimd", G_sp[tt * 128:(tt + 1) * 128, ec * 512:(ec + 1) * 512], c["TC"][:, 0:512], tag="gsp")

    att3 = c["ATT"].ap().rearrange("p (m n) -> p m n", m=8)
    vh3 = c["VH"].ap().rearrange("p (m n) -> p m n", m=16)
    ONT = c["BIGA"]
    ont3 = big3(ONT)
    for h in range(16 if phases >= 6 else 0):
        for st in range(NT):
            b.dma("scalar", vh3[:, st, :], V_sp[st * 128:(st + 1) * 128, h * 128:(h + 1) * 128], tag=f"vld{h}")
        b.dma("scalar", c["KH"][:], KT_sp[h * 128:(h + 1) * 128, :], tag=f"kld{h}")
        for tc in range(4):
            nst = 4 * tc + 4
            po = PS[2 + tc % 2]
            pd = PS[4]
            for w0 in range(0, nst, 8):
                wn = min(8, nst - w0)
                for si in range(wn):
                    st = w0 + si
                    ps = PS[si % 2]
                    b.mm(ps[:], c["KH"][:, st * 128:(st + 1) * 128], big3(QFT)[:, h, tc * 512:(tc + 1) * 512],
                         start=True, stop=True, tag="sc")
                    b.s("activation", att3[:, si, :], ps[:], AF.Square, tag="scsq")
                    j = st - 4 * tc
                    if j >= 0:
                        b.v("tensor_mul", att3[:, si, :], att3[:, si, :], c["MASKS"][:, j * 512:(j + 1) * 512], tag="msk")
                for si in range(wn):
                    st = w0 + si
                    b.mm(po[:], vh3[:, st, :], att3[:, si, :], start=(st == 0), stop=(st == nst - 1), tag="ov")
                    b.mm(pd[0:1, :], c["ONESCOL"][:], att3[:, si, :], start=(st == 0), stop=(st == nst - 1), tag="dn")
            fused_norm_evac(b, c, po, pd, ont3[:, h, tc * 512:(tc + 1) * 512], 1e-5, "onrm")

    if dbg:
        for dt in range(ND):
            b.dma("gpsimd", dbg_ot[dt * 128:(dt + 1) * 128, :], big3(ONT)[:, dt, :], tag="dbgo")

    def s_out(tt, ec, ps):
        b.v("tensor_copy", big3(c["BIGB"])[:, tt, ec * 512:(ec + 1) * 512], ps[:], tag="sev")
    if phases >= 7:
        stream_matmul_TxF(b, c, ONT, WoT, s_out, D, PS[0:2], tag="WO")

    if dbg:
        for tt in range(NT):
            b.dma("gpsimd", dbg_s[tt * 128:(tt + 1) * 128, :], big3(c["BIGB"])[:, tt, :], tag="dbgs")

    def g_src(tt):
        b.dma("sync", c["TA"][:], G_sp[tt * 128:(tt + 1) * 128, :], tag=f"gl{tt}")
        return c["TA"][:]

    def s_src(tt):
        return big3(c["BIGB"])[:, tt, :]

    if phases >= 8:
        epilogue(b, c, x_in, x_out, s_src, g_src)
    else:
        for tt in range(NT):
            b.dma("scalar", c["SCR0"][:], x_in[tt * 128:(tt + 1) * 128, :], tag="cp")
            b.dma("gpsimd", x_out[tt * 128:(tt + 1) * 128, :], c["SCR0"][:], tag="cp2")

    b.finalize()
    return nc






def build_layer0(dbg=False):
    nc = make_nc()
    x_in = nc.dram_tensor("x_in", [T, D], F32, kind="ExternalInput").ap()
    Wqkv = nc.dram_tensor("Wqkv", [D, 3 * D], F32, kind="ExternalInput").ap()
    bqkv = nc.dram_tensor("bqkv", [1, 3 * D], F32, kind="ExternalInput").ap()
    Wo = nc.dram_tensor("Wo", [D, D], F32, kind="ExternalInput").ap()
    bo = nc.dram_tensor("bo", [1, D], F32, kind="ExternalInput").ap()
    gateA = nc.dram_tensor("gateA", [44, 128, ND * 128], BF16, kind="ExternalInput").ap()
    upA = nc.dram_tensor("upA", [44, 128, ND * 128], BF16, kind="ExternalInput").ap()
    downA = nc.dram_tensor("downA", [44, 128, D], BF16, kind="ExternalInput").ap()
    x_out = nc.dram_tensor("x_out", [T, D], F32, kind="ExternalOutput").ap()

    spill_kind = "ExternalOutput" if dbg else "Internal"
    V_sp = nc.dram_tensor("V_sp", [T, D], BF16, kind=spill_kind).ap()
    QK_sp = nc.dram_tensor("QK_sp", [2 * D, T], BF16, kind=spill_kind).ap()
    G_sp = nc.dram_tensor("G_sp", [T, D], BF16, kind=spill_kind).ap()
    H_sp = nc.dram_tensor("H_sp", [NT, 44 * 128 * 128], BF16, kind="Internal").ap()
    if dbg:
        dbg_y = nc.dram_tensor("dbg_y", [T, D], BF16, kind="ExternalOutput").ap()
        dbg_ot = nc.dram_tensor("dbg_ot", [D, T], BF16, kind="ExternalOutput").ap()
        dbg_s = nc.dram_tensor("dbg_s", [T, D], BF16, kind="ExternalOutput").ap()

    b = B(nc)
    c = alloc_common(b, nc, n_big=2, skip=("VH", "QCH", "BROW"))
    XT = c["BIGA"]
    XPT = c["BIGB"]
    PS = c["PS"]

    BB = b.sb("BB", [1, D], BF16)
    BIASN = b.sb("BIASN", [128, 1], F32)
    b.v("memset", BIASN[:], -20.0, tag="lb3")
    ONES512 = b.sb("ONES512", [1, 512], BF16)
    b.v("memset", ONES512[:], 1.0, tag="lb4")

    def extra(tt, xn_tile, x_tile):
        perm_src = bass.AP(xn_tile.ap().tensor, xn_tile.ap().offset,
                           [xn_tile.ap().ap[0], [1, 8], [8, 256]])
        b.v("tensor_copy", c["TC"].ap().rearrange("p (k w) -> p k w", k=8), perm_src, tag="perm")
        b.dma_tp(big3(XPT)[:, :, tt * 128:(tt + 1) * 128], c["TC"][:], tag=f"tpp{tt}")

    prologue_ln(b, c, x_in, XT, extra_tile_fn=extra)

    b.dma("sync", c["SCR0"][0:1, :], bqkv[0:1, 2 * D:3 * D], tag="lb")
    b.v("tensor_copy", BB[:], c["SCR0"][0:1, :], tag="lb")
    wec3 = c["WEC"].ap().rearrange("p (m n) -> p m n", m=ND)
    for ec in range(4):
        for dt in range(ND):
            b.dma("gpsimd", wec3[:, dt, :],
                  Wqkv[dt * 128:(dt + 1) * 128, 2 * D + ec * 512:2 * D + (ec + 1) * 512], tag=f"vw{ec}")
        for tt in range(NT):
            ps = PS[tt % 2]
            for dt in range(ND):
                b.mm(ps[:], big3(XT)[:, dt, tt * 128:(tt + 1) * 128], wec3[:, dt, :],
                     start=(dt == 0), stop=False, tag="V")
            b.mm(ps[:], c["ONESROW"][0:1, :], BB[0:1, ec * 512:(ec + 1) * 512],
                 start=False, stop=True, tag="Vb")
            b.v("tensor_copy", c["TC"][:, 0:512], ps[:], tag="vev")
            b.dma("gpsimd", V_sp[tt * 128:(tt + 1) * 128, ec * 512:(ec + 1) * 512], c["TC"][:, 0:512], tag="vsp")

    qkl3 = c["WEC"].ap()[:, 0:ND * 128].rearrange("p (m n) -> p m n", m=ND)
    for et in range(2 * ND):
        b.dma("gpsimd", qkl3,
              Wqkv.rearrange("(m p) e -> p m e", p=128)[:, :, et * 128:(et + 1) * 128], tag=f"qkw{et}")
        b.dma("sync", c["SCR1"][0:1, 0:128], bqkv[0:1, et * 128:(et + 1) * 128], tag=f"lbe{et}")
        b.v("tensor_copy", BB[0:1, 0:128], c["SCR1"][0:1, 0:128], tag=f"lbe{et}")
        for tc in range(4):
            ps = PS[2 + tc % 2]
            for dt in range(ND):
                b.mm(ps[:], qkl3[:, dt, :], big3(XT)[:, dt, tc * 512:(tc + 1) * 512],
                     start=(dt == 0), stop=False, tag="qk")
            b.mm(ps[:], BB[0:1, 0:128], ONES512[0:1, :], start=False, stop=True, tag="qkb")
            b.v("tensor_copy", c["TC"][:, 0:512], ps[:], tag="qkev")
            b.dma("gpsimd", QK_sp[et * 128:(et + 1) * 128, tc * 512:(tc + 1) * 512], c["TC"][:, 0:512], tag="qksp")

    WECap = c["WEC"].ap()
    GW = WECap[:, 0:2048]
    UW = WECap[:, 2048:4096]
    HT = WECap[:, 4096:6144]
    gw3 = GW.rearrange("p (m n) -> p m n", m=ND)
    uw3 = UW.rearrange("p (m n) -> p m n", m=ND)
    hsp3 = H_sp.rearrange("a (it p t) -> a it p t", it=44, p=128)
    for it in range(44):
        b.dma("gpsimd", GW, gateA[it], tag=f"gw{it}")
        b.dma("gpsimd", UW, upA[it], tag=f"uw{it}")
        for tc in range(4):
            pg = PS[2]
            pu = PS[3]
            for dt in range(ND):
                b.mm(pg[:], gw3[:, dt, :], big3(XPT)[:, dt, tc * 512:(tc + 1) * 512],
                     start=(dt == 0), stop=(dt == ND - 1), tag="gmm")
            for dt in range(ND):
                b.mm(pu[:], uw3[:, dt, :], big3(XPT)[:, dt, tc * 512:(tc + 1) * 512],
                     start=(dt == 0), stop=(dt == ND - 1), tag="umm")
            b.s("activation", c["TC"][:, 0:512], pg[:], AF.Silu, tag="hsilu")
            b.v("tensor_tensor", HT[:, tc * 512:(tc + 1) * 512], pu[:], c["TC"][:, 0:512], ALU.mult, tag="hmul")
        for tt in range(NT):
            b.dma("gpsimd", hsp3[tt, it], HT[:, tt * 128:(tt + 1) * 128], tag=f"hsp{it}")

    xptf = XPT.ap()
    for ec in range(4):
        for it in range(44):
            b.dma("sync", xptf[:, it * 512:(it + 1) * 512], downA[it, :, ec * 512:(ec + 1) * 512], tag=f"dw{ec}")
        for tt in range(NT):
            b.dma("scalar", xptf[:, 44 * 512:44 * 512 + 44 * 128].rearrange("p (it t) -> p it t", it=44),
                  H_sp.rearrange("a (it p t) -> a p it t", it=44, p=128)[tt], tag=f"hl{ec}_{tt}")
            ps = PS[4 + tt % 2]
            for it in range(44):
                b.mm(ps[:], xptf[:, 44 * 512 + it * 128:44 * 512 + (it + 1) * 128],
                     xptf[:, it * 512:(it + 1) * 512], start=(it == 0), stop=(it == 43), tag="dmm")
            b.v("tensor_copy", big3(XT)[:, tt, ec * 512:(ec + 1) * 512], ps[:], tag="yev")

    for tt in range(NT):
        y = big3(XT)[:, tt, :]
        if dbg:
            b.dma("gpsimd", dbg_y[tt * 128:(tt + 1) * 128, :], y, tag="dbgy")
        b.s("activation", c["TC"][:], y, AF.Square, accum_out=c["SM0"][:, tt % 64:tt % 64 + 1], tag="rms")
        b.v("tensor_scalar", c["SM1"][:, tt % 64:tt % 64 + 1], c["SM0"][:, tt % 64:tt % 64 + 1], 1.0 / D, 1e-6, ALU.mult, ALU.add, tag="rms")
        b.s("activation", c["SM1"][:, tt % 64:tt % 64 + 1], c["SM1"][:, tt % 64:tt % 64 + 1], AF.Ln, tag="rms")
        b.s("activation", c["SM2"][:, tt % 64:tt % 64 + 1], c["SM1"][:, tt % 64:tt % 64 + 1], AF.Exp, scale=-0.5, tag="rms")
        b.s("activation", c["TD"][:], y, AF.Silu, scale=c["SM2"][:, tt % 64:tt % 64 + 1], tag="gsil")
        b.dma("gpsimd", G_sp[tt * 128:(tt + 1) * 128, :], c["TD"][:], tag="gsp")

    att3 = c["ATT"].ap().rearrange("p (m n) -> p m n", m=8)
    KH = c["KH"]
    QH = c["WEC"].ap()[:, 0:T]
    VH = c["WEC"].ap()[:, T:T + 16 * 128]
    vh3 = VH.rearrange("p (m n) -> p m n", m=16)
    ONT = XT
    ont3 = big3(ONT)
    SC128 = 2048.0 ** -0.5 * 16.0 ** 0.5
    for h in range(16):
        b.dma("scalar", KH[:], QK_sp[D + h * 128:D + (h + 1) * 128, :], tag=f"kld{h}")
        b.dma("scalar", QH, QK_sp[h * 128:(h + 1) * 128, :], tag=f"qld{h}")
        for st in range(NT):
            b.dma("scalar", vh3[:, st, :], V_sp[st * 128:(st + 1) * 128, h * 128:(h + 1) * 128], tag=f"vld{h}")
        for tc in range(4):
            nst = 4 * tc + 4
            po = PS[2 + tc % 2]
            pd = c["PSS"]
            for w0 in range(0, nst, 8):
                wn = min(8, nst - w0)
                for si in range(wn):
                    st = w0 + si
                    ps = PS[st % 2]
                    b.mm(ps[:], KH[:, st * 128:(st + 1) * 128], QH[:, tc * 512:(tc + 1) * 512],
                         start=True, stop=True, tag="sc")
                    b.s("activation", att3[:, si, :], ps[:], AF.Exp, scale=SC128, bias=BIASN[:], tag="scexp")
                    j = st - 4 * tc
                    if j >= 0:
                        b.v("tensor_mul", att3[:, si, :], att3[:, si, :], c["MASKS"][:, j * 512:(j + 1) * 512], tag="msk")
                for si in range(wn):
                    st = w0 + si
                    b.mm(po[:], vh3[:, st, :], att3[:, si, :], start=(st == 0), stop=(st == nst - 1), tag="ov")
                    b.mm(pd[0:1, :], c["ONESCOL"][:], att3[:, si, :], start=(st == 0), stop=(st == nst - 1), tag="dn")
            b.v("tensor_copy", c["DSTG"][:], pd[0:1, :], tag="onrm")
            b.v("reciprocal", c["DSTG"][:], c["DSTG"][:], tag="onrm")
            b.v("tensor_copy", c["DSTGB"][:], c["DSTG"][:], tag="onrm")
            b.mm(c["PSR"][:], c["ONESROW"][0:1, :], c["DSTGB"][:], start=True, stop=True, tag="onrm")
            b.v("tensor_copy", c["RSB"][:], c["PSR"][:], tag="onrm")
            b.v("tensor_tensor", ont3[:, h, tc * 512:(tc + 1) * 512], po[:], c["RSB"][:], ALU.mult, tag="onrm2")
    if dbg:
        for et in range(ND):
            b.dma("gpsimd", dbg_ot[et * 128:(et + 1) * 128, :], big3(ONT)[:, et, :], tag="dbgo")

    b.dma("sync", c["SCR0"][0:1, :], bo, tag="lbo")
    b.v("tensor_copy", BB[:], c["SCR0"][0:1, :], tag="lbo")

    def s_out(tt, ec, ps):
        b.v("tensor_copy", big3(XPT)[:, tt, ec * 512:(ec + 1) * 512], ps[:], tag="sev")
    stream_matmul_TxF(b, c, ONT, Wo, s_out, D, PS[0:2],
                      bias_row=lambda ec: BB[0:1, ec * 512:(ec + 1) * 512], tag="WO")
    if dbg:
        for tt in range(NT):
            b.dma("gpsimd", dbg_s[tt * 128:(tt + 1) * 128, :], big3(XPT)[:, tt, :], tag="dbgs")

    def g_src(tt):
        b.dma("sync", c["TA"][:], G_sp[tt * 128:(tt + 1) * 128, :], tag=f"gl{tt}")
        return c["TA"][:]

    def s_src(tt):
        return big3(XPT)[:, tt, :]

    epilogue(b, c, x_in, x_out, s_src, g_src)

    b.finalize()
    return nc



def build_layer2(dbg=False):
    nc = make_nc()
    x_in = nc.dram_tensor("x_in", [T, D], F32, kind="ExternalInput").ap()
    WinA = nc.dram_tensor("WinA", [69, 128, ND * 128], BF16, kind="ExternalInput").ap()
    Wh2iT = nc.dram_tensor("Wh2iT", [16, 4096], BF16, kind="ExternalInput").ap()
    WoutA = nc.dram_tensor("WoutA", [32, 128, D], BF16, kind="ExternalInput").ap()
    tgA = nc.dram_tensor("tgA", [44, 128, ND * 128], BF16, kind="ExternalInput").ap()
    tuA = nc.dram_tensor("tuA", [44, 128, ND * 128], BF16, kind="ExternalInput").ap()
    tdA = nc.dram_tensor("tdA", [44, 128, D], BF16, kind="ExternalInput").ap()
    wconv = nc.dram_tensor("wconv", [32, 128, 4], F32, kind="ExternalInput").ap()
    wnorm_col = nc.dram_tensor("wnorm_col", [128, ND], F32, kind="ExternalInput").ap()
    dtb = nc.dram_tensor("dtb", [16, 1], F32, kind="ExternalInput").ap()
    alog = nc.dram_tensor("alog", [16, 1], F32, kind="ExternalInput").ap()
    x_out = nc.dram_tensor("x_out", [T, D], F32, kind="ExternalOutput").ap()

    spill_kind = "ExternalOutput" if dbg else "Internal"
    BCDT_sp = nc.dram_tensor("BCDT_sp", [5 * 128, T], F32, kind=spill_kind).ap()
    YG_sp = nc.dram_tensor("YG_sp", [NT, 32 * 128 * 128], BF16, kind="Internal").ap()
    S2_sp = nc.dram_tensor("S2_sp", [T, D], BF16, kind=spill_kind).ap()
    H2_sp = nc.dram_tensor("H2_sp", [NT, 44 * 128 * 128], BF16, kind="Internal").ap()
    if dbg:
        dbg_y16 = nc.dram_tensor("dbg_y16", [16, T], F32, kind="ExternalOutput").ap()
        dbg_xm = nc.dram_tensor("dbg_xm", [16, T], F32, kind="ExternalOutput").ap()
        dbg_dt = nc.dram_tensor("dbg_dt", [16, T], F32, kind="ExternalOutput").ap()
        dbg_ffny = nc.dram_tensor("dbg_ffny", [T, D], BF16, kind="ExternalOutput").ap()

    b = B(nc)
    c = alloc_common(b, nc, n_big=2, skip=("VH", "QCH", "BROW", "ATT", "KH"))
    XT = c["BIGA"]
    XN2T = c["BIGB"]
    PS = c["PS"]
    WECF = c["WEC"].bitcast(F32)
    SCR2 = WECF.ap()[:, 2048:4096]
    DTT = b.sb("DTT", [16, T], F32).ap()
    T16A = b.sb("T16A", [16, T], BF16).ap()
    T16B = b.sb("T16B", [16, T], BF16).ap()
    YB = c["WEC"]
    YBv = c["WEC"].ap()[0:16, 2048:4096]
    W2IS = b.sb("W2IS", [16, 128], BF16)
    WNC = b.sb("WNC", [128, ND], F32)
    WCV = b.sb("WCV", [128, 32 * 4], F32)
    DTB = b.sb("DTB", [16, 1], F32)
    ALC = b.sb("ALC", [16, 1], F32)
    eh = np.zeros((32 * 128, 16), dtype=BF)
    for ct in range(32):
        eh[ct * 128:(ct + 1) * 128, ct // 2] = 1.0
    EHc = nc.inline_tensor(eh, "c_eh")
    EH = b.sb("EH", [128, 32 * 16], BF16)
    for ct in range(32):
        b.dma("sync", EH[:, ct * 16:(ct + 1) * 16], EHc.ap()[ct * 128:(ct + 1) * 128, :], tag="ceh")
    oh = np.zeros((2 * 128, 16), dtype=BF)
    rep = np.zeros((32, 128), dtype=np.float32)
    for half in range(2):
        for p in range(128):
            oh[half * 128 + p, 8 * half + p // 16] = 1.0
        for r in range(16):
            for p in range(128):
                if r == 8 * half + p // 16:
                    rep[half * 16 + r, p] = 1.0
    OHc = nc.inline_tensor(oh, "c_oh")
    REPBc = nc.inline_tensor(rep.astype(BF), "c_repb")
    OH = b.sb("OH", [128, 2 * 16], BF16)
    REPB = b.sb("REPB", [16, 2 * 128], BF16)
    for half in range(2):
        b.dma("sync", OH[:, half * 16:(half + 1) * 16], OHc.ap()[half * 128:(half + 1) * 128, :], tag="coh")
        b.dma("sync", REPB[:, half * 128:(half + 1) * 128], REPBc.ap()[half * 16:(half + 1) * 16, :], tag="crepb")
    b.dma("sync", WNC[:], wnorm_col, tag="cwn")
    b.dma("sync", WCV.ap().rearrange("p (a k) -> p a k", a=32), wconv.rearrange("a p k -> p a k"), tag="cwc")
    b.dma("sync", DTB[:], dtb, tag="cdtb")
    b.dma("sync", ALC[:], alog, tag="cal")

    VARC = b.sb("VARC", [128, NT], F32)

    def extra(tt, xn_tile, x_tile):
        co = tt % 64
        b.v("tensor_scalar_add", c["SM0"][:, co:co + 1], VARC[:, tt:tt + 1], 1e-5, tag="r2")
        b.v("reciprocal", c["SM0"][:, co:co + 1], c["SM0"][:, co:co + 1], tag="r2")
        b.v("tensor_tensor", c["SM0"][:, co:co + 1], c["SM0"][:, co:co + 1], VARC[:, tt:tt + 1], ALU.mult, tag="r2")
        b.v("tensor_scalar_add", c["SM0"][:, co:co + 1], c["SM0"][:, co:co + 1], 1.1920929e-07, tag="r2")
        b.s("activation", c["SM0"][:, co:co + 1], c["SM0"][:, co:co + 1], AF.Ln, tag="r2")
        b.s("activation", c["SM0"][:, co:co + 1], c["SM0"][:, co:co + 1], AF.Exp, scale=-0.5, tag="r2")
        b.v("tensor_single_scalar", c["TC"][:], xn_tile.ap() if hasattr(xn_tile, "ap") else xn_tile, c["SM0"][:, co:co + 1], ALU.mult, tag="x2")
        b.dma_tp(big3(XN2T)[:, :, tt * 128:(tt + 1) * 128], c["TC"][:], tag=f"tp2{tt}")

    prologue_ln(b, c, x_in, XT, extra_tile_fn=extra, save_var=VARC)
    for dt_ in range(ND):
        b.v("tensor_single_scalar", big3(XN2T)[:, dt_, :], big3(XN2T)[:, dt_, :], WNC[:, dt_:dt_ + 1], ALU.mult, tag="wn")

    wec = c["WEC"].ap()
    wl3 = wec[:, 0:2048].rearrange("p (m n) -> p m n", m=ND)

    for ft in range(64, 69):
        b.dma("gpsimd", wl3, WinA[ft], tag=f"aw{ft}")
        for tc in range(4):
            ps = PS[2 + tc % 2]
            for dt_ in range(ND):
                b.mm(ps[:], wl3[:, dt_, :], big3(XN2T)[:, dt_, tc * 512:(tc + 1) * 512],
                     start=(dt_ == 0), stop=(dt_ == ND - 1), tag="amm")
            b.v("tensor_copy", SCR2[:, tc * 512:(tc + 1) * 512], ps[:], tag="aev")
        b.dma("gpsimd", BCDT_sp[(ft - 64) * 128:(ft - 63) * 128, :], SCR2, tag=f"asp{ft}")

    pxm = [PS[4], PS[5], c["PSR"], c["PSS"]]
    for ct in range(32):
        b.dma("gpsimd", wl3, WinA[32 + ct], tag=f"bw{ct}")
        for tc in range(4):
            ps = PS[2 + tc % 2]
            for dt_ in range(ND):
                b.mm(ps[:], wl3[:, dt_, :], big3(XN2T)[:, dt_, tc * 512:(tc + 1) * 512],
                     start=(dt_ == 0), stop=(dt_ == ND - 1), tag="bmm")
            b.v("tensor_copy", SCR2[:, tc * 512:(tc + 1) * 512], ps[:], tag="bev")
        w3 = WCV[:, ct * 4 + 3:ct * 4 + 4]
        w2 = WCV[:, ct * 4 + 2:ct * 4 + 3]
        w1 = WCV[:, ct * 4 + 1:ct * 4 + 2]
        w0 = WCV[:, ct * 4 + 0:ct * 4 + 1]
        b.v("tensor_single_scalar", c["SCR0"][:], SCR2, w3, ALU.mult, tag="cv")
        b.v("scalar_tensor_tensor", c["SCR0"][:, 1:2048], SCR2[:, 0:2047], w2, c["SCR0"][:, 1:2048], ALU.mult, ALU.add, tag="cv")
        b.v("scalar_tensor_tensor", c["SCR0"][:, 2:2048], SCR2[:, 0:2046], w1, c["SCR0"][:, 2:2048], ALU.mult, ALU.add, tag="cv")
        b.v("scalar_tensor_tensor", c["SCR0"][:, 3:2048], SCR2[:, 0:2045], w0, c["SCR0"][:, 3:2048], ALU.mult, ALU.add, tag="cv")
        b.s("activation", c["TC"][:], c["SCR0"][:], AF.Silu, tag="cvs")
        for tc in range(4):
            b.mm(pxm[tc][0:16, :], EH[:, ct * 16:(ct + 1) * 16], c["TC"][:, tc * 512:(tc + 1) * 512],
                 start=(ct == 0), stop=(ct == 31), tag="xm")
    for tc in range(4):
        b.v("tensor_copy", T16B[:, tc * 512:(tc + 1) * 512], pxm[tc][0:16, :], tag="xmev")
    if dbg:
        b.dma("gpsimd", dbg_xm, T16B, tag="dxm")

    b.dma("scalar", DTT, BCDT_sp[0:16, :], tag="lddt")
    b.v("tensor_single_scalar", DTT, DTT, DTB[:], ALU.add, tag="sp")
    spx = SCR2[0:16, :]
    b.s("activation", spx, DTT, AF.Abs, tag="sp")
    b.s("activation", spx, spx, AF.Exp, scale=-1.0, tag="sp")
    b.v("tensor_scalar_add", spx, spx, 1.0, tag="sp")
    b.s("activation", spx, spx, AF.Ln, tag="sp")
    b.v("tensor_scalar_max", DTT, DTT, 0.0, tag="sp")
    b.v("tensor_add", DTT, DTT, spx, tag="sp")
    if dbg:
        b.dma("gpsimd", dbg_dt, DTT, tag="ddt")
    b.s("activation", ALC[:], ALC[:], AF.Exp, tag="sp2")
    b.v("tensor_scalar_mul", ALC[:], ALC[:], -1.0, tag="sp2")
    b.v("tensor_single_scalar", T16A, DTT, ALC[:], ALU.mult, tag="sp2")
    b.v("tensor_tensor", T16B, T16B, DTT, ALU.mult, tag="sp3")
    b.v("tensor_scalar_mul", T16B, T16B, 1.0 / 256, tag="sp3")
    for half in range(2):
        for tc in range(4):
            pa = PS[2 + tc % 2]
            b.mm(pa[:], REPB[0:16, half * 128:(half + 1) * 128], T16A[:, tc * 512:(tc + 1) * 512],
                 start=True, stop=True, tag="rep")
            b.s("activation", c["SCR0"][:, tc * 512:(tc + 1) * 512], pa[:], AF.Exp, tag="repev")
            pb = PS[2 + (tc + 1) % 2]
            b.mm(pb[:], REPB[0:16, half * 128:(half + 1) * 128], T16B[:, tc * 512:(tc + 1) * 512],
                 start=True, stop=True, tag="rep2")
            b.v("tensor_copy", c["SCR1"][:, tc * 512:(tc + 1) * 512], pb[:], tag="rep2ev")
        b.dma("gpsimd", c["TD"][:], BCDT_sp[16 + half * 128:16 + (half + 1) * 128, :], tag=f"ldB{half}")
        b.v("tensor_mul", c["SCR1"][:], c["SCR1"][:], c["TD"][:], tag="umul")
        b.v("tensor_tensor_scan", SCR2, c["SCR0"][:], c["SCR1"][:], 0.0, ALU.mult, ALU.add, tag="scan")
        b.dma("gpsimd", c["TD"][:], BCDT_sp[272 + half * 128:272 + (half + 1) * 128, :], tag=f"ldC{half}")
        b.v("tensor_mul", SCR2, SCR2, c["TD"][:], tag="hC")
        b.v("tensor_copy", c["TC"][:], SCR2, tag="hCb")
        for tc in range(4):
            b.mm(pxm[tc][0:16, :], OH[:, half * 16:(half + 1) * 16], c["TC"][:, tc * 512:(tc + 1) * 512],
                 start=(half == 0), stop=(half == 1), tag="ysum")
    for tc in range(4):
        b.v("tensor_copy", YBv[:, tc * 512:(tc + 1) * 512], pxm[tc][0:16, :], tag="yev")
    if dbg:
        b.dma("gpsimd", dbg_y16, YBv, tag="dy16")

    yg3 = YG_sp.rearrange("a (it p t) -> a it p t", it=32, p=128)
    for it in range(32):
        b.dma("gpsimd", wl3, WinA[it], tag=f"zw{it}")
        b.dma("gpsimd", W2IS[:], Wh2iT[:, it * 128:(it + 1) * 128], tag=f"wh{it}")
        for tc in range(4):
            pz = PS[2 + tc % 2]
            for dt_ in range(ND):
                b.mm(pz[:], wl3[:, dt_, :], big3(XN2T)[:, dt_, tc * 512:(tc + 1) * 512],
                     start=(dt_ == 0), stop=(dt_ == ND - 1), tag="zmm")
            py = PS[4 + tc % 2]
            b.mm(py[:], W2IS[:], YBv[:, tc * 512:(tc + 1) * 512], start=True, stop=True, tag="yy")
            b.s("activation", c["TC"][:, 0:512], pz[:], AF.Silu, tag="zsil")
            b.v("tensor_tensor", c["TD"][:, tc * 512:(tc + 1) * 512], py[:], c["TC"][:, 0:512], ALU.mult, tag="ygm")
        for tt in range(NT):
            b.dma("gpsimd", yg3[tt, it], c["TD"][:, tt * 128:(tt + 1) * 128], tag=f"ygsp{it}")
    xb = XN2T.ap()
    for ec in range(4):
        for it in range(32):
            b.dma("sync", xb[:, it * 512:(it + 1) * 512], WoutA[it, :, ec * 512:(ec + 1) * 512], tag=f"ew{ec}")
        for tt in range(NT):
            b.dma("scalar", xb[:, 32 * 512:32 * 512 + 32 * 128].rearrange("p (it t) -> p it t", it=32),
                  YG_sp.rearrange("a (it p t) -> a p it t", it=32, p=128)[tt], tag=f"eyg{ec}_{tt}")
            ps = PS[2 + tt % 2]
            for it in range(32):
                b.mm(ps[:], xb[:, 32 * 512 + it * 128:32 * 512 + (it + 1) * 128],
                     xb[:, it * 512:(it + 1) * 512], start=(it == 0), stop=(it == 31), tag="emm")
            b.v("tensor_copy", c["TC"][:, 0:512], ps[:], tag="eev")
            b.dma("gpsimd", S2_sp[tt * 128:(tt + 1) * 128, ec * 512:(ec + 1) * 512], c["TC"][:, 0:512], tag="esp")

    GW = wec[:, 0:2048]
    UW = wec[:, 2048:4096]
    HT = wec[:, 4096:6144]
    gw3 = GW.rearrange("p (m n) -> p m n", m=ND)
    uw3 = UW.rearrange("p (m n) -> p m n", m=ND)
    h2a = H2_sp.rearrange("a (it p t) -> a it p t", it=44, p=128)
    for it in range(44):
        b.dma("gpsimd", GW, tgA[it], tag=f"fg{it}")
        b.dma("gpsimd", UW, tuA[it], tag=f"fu{it}")
        for tc in range(4):
            pg = PS[2]
            pu = PS[3]
            for dt_ in range(ND):
                b.mm(pg[:], gw3[:, dt_, :], big3(XT)[:, dt_, tc * 512:(tc + 1) * 512],
                     start=(dt_ == 0), stop=(dt_ == ND - 1), tag="fgmm")
            for dt_ in range(ND):
                b.mm(pu[:], uw3[:, dt_, :], big3(XT)[:, dt_, tc * 512:(tc + 1) * 512],
                     start=(dt_ == 0), stop=(dt_ == ND - 1), tag="fumm")
            b.s("activation", c["TC"][:, 0:512], pg[:], AF.Silu, tag="fsil")
            b.v("tensor_tensor", HT[:, tc * 512:(tc + 1) * 512], pu[:], c["TC"][:, 0:512], ALU.mult, tag="fmul")
        for tt in range(NT):
            b.dma("gpsimd", h2a[tt, it], HT[:, tt * 128:(tt + 1) * 128], tag=f"fh{it}")
    for ec in range(4):
        for it in range(44):
            b.dma("sync", xb[:, it * 512:(it + 1) * 512], tdA[it, :, ec * 512:(ec + 1) * 512], tag=f"fd{ec}")
        for tt in range(NT):
            b.dma("scalar", xb[:, 44 * 512:44 * 512 + 44 * 128].rearrange("p (it t) -> p it t", it=44),
                  H2_sp.rearrange("a (it p t) -> a p it t", it=44, p=128)[tt], tag=f"fh2{ec}_{tt}")
            ps = PS[4 + tt % 2]
            for it in range(44):
                b.mm(ps[:], xb[:, 44 * 512 + it * 128:44 * 512 + (it + 1) * 128],
                     xb[:, it * 512:(it + 1) * 512], start=(it == 0), stop=(it == 43), tag="fdmm")
            b.v("tensor_copy", big3(XT)[:, tt, ec * 512:(ec + 1) * 512], ps[:], tag="fyev")
    if dbg:
        for tt in range(NT):
            b.dma("gpsimd", dbg_ffny[tt * 128:(tt + 1) * 128, :], big3(XT)[:, tt, :], tag="dffny")

    for tt in range(NT):
        xbuf = c["SCR0"] if tt % 2 == 0 else c["SCR1"]
        b.dma("scalar", xbuf[:], x_in[tt * 128:(tt + 1) * 128, :], tag=f"ex{tt}")
        b.dma("sync", c["TA"][:], S2_sp[tt * 128:(tt + 1) * 128, :], tag=f"es{tt}")
        b.s("activation", c["TB"][:], big3(XT)[:, tt, :], AF.Silu, tag="eg")
        b.v("tensor_scalar", c["TC"][:], xbuf[:], c["MCOL"][:, tt:tt + 1], c["RCOL"][:, tt:tt + 1], ALU.subtract, ALU.mult, tag="exn")
        b.v("tensor_add", c["TC"][:], c["TC"][:], c["TA"][:], tag="exs")
        b.v("tensor_tensor", c["TD"][:], c["TC"][:], c["TB"][:], ALU.mult, tag="esg")
        b.v("tensor_add", xbuf[:], xbuf[:], c["TD"][:], tag="exx")
        b.dma("gpsimd", x_out[tt * 128:(tt + 1) * 128, :], xbuf[:], tag=f"sx{tt}")

    b.finalize()
    return nc



def _alibi_slopes():
    import math
    start = 2.0 ** (-2.0 ** (-(math.log2(16) - 3)))
    return [start * start ** i for i in range(16)]


def build_layer3a():
    nc = make_nc()
    x_in = nc.dram_tensor("x_in", [T, D], F32, kind="ExternalInput").ap()
    m_out = nc.dram_tensor("m_out", [1, 1], F32, kind="ExternalOutput").ap()
    b = B(nc)
    idf = nc.inline_tensor(np.eye(128, dtype=np.float32), "c_idf")
    ID = b.sb("ID", [128, 128], F32)
    b.dma("sync", ID[:], idf.ap(), tag="cid")
    SCRA = b.sb("SCRA", [128, 2048], F32)
    SCRB = b.sb("SCRB", [128, 2048], F32)
    SM = b.sb("SM", [128, 20], F32)
    ROW = b.sb("ROW", [1, 128], F32)
    PT = b.ps("pt", [128, 128])
    for tt in range(NT):
        b.dma("sync", SCRA[:], x_in[tt * 128:(tt + 1) * 128, :], tag=f"lx{tt}")
        b.v("reduce_sum", SM[:, 16:17], SCRA[:], axis=AX.X, tag="m")
        b.v("tensor_scalar_mul", SM[:, 16:17], SM[:, 16:17], 1.0 / D, tag="m")
        b.v("tensor_single_scalar", SCRA[:], SCRA[:], SM[:, 16:17], ALU.subtract, tag="c")
        b.s("activation", SCRB[:], SCRA[:], AF.Square, accum_out=SM[:, 17:18], tag="sq")
        b.v("tensor_scalar", SM[:, 17:18], SM[:, 17:18], 1.0 / D, 1e-5, ALU.mult, ALU.add, tag="r")
        b.s("activation", SM[:, 17:18], SM[:, 17:18], AF.Ln, tag="r")
        b.s("activation", SM[:, 17:18], SM[:, 17:18], AF.Exp, scale=-0.5, tag="r")
        b.v("reduce_max", SM[:, 18:19], SCRA[:], axis=AX.X, apply_absolute_value=True, tag="mx")
        b.v("tensor_mul", SM[:, tt:tt + 1], SM[:, 18:19], SM[:, 17:18], tag="mx")
    b.v("reduce_max", SM[:, 19:20], SM[:, 0:16], axis=AX.X, tag="fin")
    b.op("tensor", lambda eng: eng.transpose(PT[0:1, 0:128], SM[:, 19:20], ID[:]),
         [SM[:], ID[:]], [PT[:]], tag="tp")
    b.v("reduce_max", ROW[0:1, 0:1], PT[0:1, 0:128], axis=AX.X, tag="fin3")
    b.dma("gpsimd", m_out, ROW[0:1, 0:1], tag="out")
    b.finalize()
    return nc


def build_layer3b(dbg=False):
    nc = make_nc()
    x_in = nc.dram_tensor("x_in", [T, D], F32, kind="ExternalInput").ap()
    Wq = nc.dram_tensor("Wq", [D, D], F32, kind="ExternalInput").ap()
    Wk = nc.dram_tensor("Wk", [D, 512], F32, kind="ExternalInput").ap()
    Wv = nc.dram_tensor("Wv", [D, 512], F32, kind="ExternalInput").ap()
    Wo = nc.dram_tensor("Wo", [D, D], F32, kind="ExternalInput").ap()
    qw1A = nc.dram_tensor("qw1A", [64, 128, ND * 128], BF16, kind="ExternalInput").ap()
    sw1 = nc.dram_tensor("sw1", [128, 64], F32, kind="ExternalInput").ap()
    b1 = nc.dram_tensor("b1", [128, 64], F32, kind="ExternalInput").ap()
    gmax8 = nc.dram_tensor("gmax8", [1, 8], F32, kind="ExternalInput").ap()
    s3_out = nc.dram_tensor("s3_out", [T, D], BF16, kind="ExternalOutput").ap()
    h_out = nc.dram_tensor("h_out", [64, 128, T], BF16, kind="ExternalOutput").ap()
    hm_out = nc.dram_tensor("hm_out", [1, 1], F32, kind="ExternalOutput").ap()
    QT_sp = nc.dram_tensor("QT_sp", [D, T], BF16, kind="Internal").ap()
    KT_sp = nc.dram_tensor("KT_sp", [512, T], BF16, kind="Internal").ap()
    V_sp = nc.dram_tensor("V_sp", [T, 512], BF16, kind="Internal").ap()
    if dbg:
        dbg_ot = nc.dram_tensor("dbg_ot", [D, T], BF16, kind="ExternalOutput").ap()

    b = B(nc)
    c = alloc_common(b, nc, n_big=2, skip=("VH", "QCH", "BROW"))
    XT = c["BIGA"]
    ONT = c["BIGB"]
    PS = c["PS"]
    slopes = _alibi_slopes()
    iota = nc.inline_tensor(np.arange(128, dtype=np.float32).reshape(128, 1), "c_iota")
    IOTA = b.sb("IOTA", [128, 1], F32)
    b.dma("sync", IOTA[:], iota.ap(), tag="cio")
    wm = np.zeros((4 * 128, 512), dtype=BF)
    pidx = np.arange(128)[:, None]
    fidx = np.arange(512)[None, :]
    for j in range(4):
        wm[j * 128:(j + 1) * 128] = ((fidx - pidx) <= 128 * j).astype(BF)
    wmc = nc.inline_tensor(wm, "c_wmasks")
    WMASKS = b.sb("WMASKS", [128, 4 * 512], BF16)
    for j in range(4):
        b.dma("sync", WMASKS[:, j * 512:(j + 1) * 512], wmc.ap()[j * 128:(j + 1) * 128, :], tag="cwm")
    BIASC = b.sb("BIASC", [128, 1], F32)
    b.v("memset", BIASC[:], -20.0, tag="biasc")
    MXC = b.sb("MXC", [128, 64], F32)

    prologue_ln(b, c, x_in, XT)

    wec3 = c["WEC"].ap()[:, 0:ND * 128].rearrange("p (m n) -> p m n", m=ND)
    def proj_T(w_dram, n_rows, sp, tag):
        for et in range(n_rows // 128):
            b.dma("gpsimd", wec3, w_dram.rearrange("(m p) e -> p m e", p=128)[:, :, et * 128:(et + 1) * 128], tag=f"{tag}w{et}")
            for tc in range(4):
                ps = PS[2 + tc % 2]
                for dt_ in range(ND):
                    b.mm(ps[:], wec3[:, dt_, :], big3(XT)[:, dt_, tc * 512:(tc + 1) * 512],
                         start=(dt_ == 0), stop=(dt_ == ND - 1), tag=tag)
                b.v("tensor_copy", c["TC"][:, 0:512], ps[:], tag=f"{tag}ev")
                b.dma("gpsimd", sp[et * 128:(et + 1) * 128, tc * 512:(tc + 1) * 512], c["TC"][:, 0:512], tag=f"{tag}sp")
    proj_T(Wq, D, QT_sp, "q3")
    proj_T(Wk, 512, KT_sp, "k3")
    wv3 = c["WEC"].ap()[:, 0:ND * 512].rearrange("p (m n) -> p m n", m=ND)
    for dt_ in range(ND):
        b.dma("gpsimd", wv3[:, dt_, :], Wv[dt_ * 128:(dt_ + 1) * 128, :], tag="v3w")
    for tt in range(NT):
        ps = PS[tt % 2]
        for dt_ in range(ND):
            b.mm(ps[:], big3(XT)[:, dt_, tt * 128:(tt + 1) * 128], wv3[:, dt_, :],
                 start=(dt_ == 0), stop=(dt_ == ND - 1), tag="v3")
        b.v("tensor_copy", c["TC"][:, 0:512], ps[:], tag="v3ev")
        b.dma("gpsimd", V_sp[tt * 128:(tt + 1) * 128, :], c["TC"][:, 0:512], tag="v3sp")

    att3 = c["ATT"].ap().rearrange("p (m n) -> p m n", m=8)
    KH = c["KH"]
    QH = c["WEC"].ap()[:, 0:T]
    VH = c["WEC"].ap()[:, T:T + 16 * 128]
    vh3 = VH.rearrange("p (m n) -> p m n", m=16)
    SC128 = 128.0 ** -0.5
    ont3 = big3(ONT)
    HW = [512] * 16
    for h in range(16):
        g = h // 4
        w = HW[h]
        b.dma("scalar", KH[:], KT_sp[g * 128:(g + 1) * 128, :], tag=f"kld{h}")
        b.dma("scalar", QH, QT_sp[h * 128:(h + 1) * 128, :], tag=f"qld{h}")
        for st in range(NT):
            b.dma("scalar", vh3[:, st, :], V_sp[st * 128:(st + 1) * 128, g * 128:(g + 1) * 128], tag=f"vld{h}")
        for ci in range(T // w):
            t0 = ci * w
            t_ref = t0 + w - 1
            st0 = max(0, (t0 - 1024) // 128)
            sts = list(range(st0, (t0 + w - 1) // 128 + 1))
            po = PS[2 + ci % 2]
            pd = c["PSS"]
            for w0 in range(0, len(sts), 8):
                wave = sts[w0:w0 + 8]
                for si, st in enumerate(wave):
                    ps = PS[st % 2]
                    b.mm(ps[:, 0:w], KH[:, st * 128:(st + 1) * 128], QH[:, t0:t0 + w],
                         start=True, stop=True, tag="sc")
                    b.s("activation", att3[:, si, 0:w], ps[:, 0:w], AF.Exp, scale=SC128, bias=BIASC[:], tag="scexp")
                    cc = t0 - 128 * st
                    if cc < 127:
                        j = 0 if cc >= 0 else (-cc + 127) // 128
                        off2 = cc + 128 * j
                        assert 0 <= j <= 3 and 0 <= off2 and off2 + w <= 512, (j, off2, w)
                        b.v("tensor_mul", att3[:, si, 0:w], att3[:, si, 0:w],
                            c["MASKS"][:, j * 512 + off2:j * 512 + off2 + w], tag="msk")
                    kk = 1024 - t0 + 128 * st
                    if kk < w - 1:
                        k = (kk + 127) // 128
                        off3 = 128 * k - kk
                        assert 0 <= k <= 3 and off3 + w <= 512, (k, off3, w)
                        b.v("tensor_mul", att3[:, si, 0:w], att3[:, si, 0:w],
                            WMASKS[:, k * 512 + off3:k * 512 + off3 + w], tag="wmsk")
                for si, st in enumerate(wave):
                    b.mm(po[:, 0:w], vh3[:, st, :], att3[:, si, 0:w], start=(st == sts[0]), stop=(st == sts[-1]), tag="ov")
                    b.mm(pd[0:1, 0:w], c["ONESCOL"][:], att3[:, si, 0:w], start=(st == sts[0]), stop=(st == sts[-1]), tag="dn")
            b.v("tensor_copy", c["DSTG"][0:1, 0:w], pd[0:1, 0:w], tag="onrm")
            b.v("reciprocal", c["DSTG"][0:1, 0:w], c["DSTG"][0:1, 0:w], tag="onrm")
            b.v("tensor_copy", c["DSTGB"][0:1, 0:w], c["DSTG"][0:1, 0:w], tag="onrm")
            b.mm(c["PSR"][:, 0:w], c["ONESROW"][0:1, :], c["DSTGB"][0:1, 0:w], start=True, stop=True, tag="onrm")
            b.v("tensor_copy", c["RSB"][:, 0:w], c["PSR"][:, 0:w], tag="onrm")
            b.v("tensor_tensor", ont3[:, h, t0:t0 + w], po[:, 0:w], c["RSB"][:, 0:w], ALU.mult, tag="onrm2")
    if dbg:
        for et in range(ND):
            b.dma("gpsimd", dbg_ot[et * 128:(et + 1) * 128, :], ont3[:, et, :], tag="dbgo")

    def s_out(tt, ec, ps):
        b.v("tensor_copy", c["TC"][:, 0:512], ps[:], tag="sev")
        b.dma("gpsimd", s3_out[tt * 128:(tt + 1) * 128, ec * 512:(ec + 1) * 512], c["TC"][:, 0:512], tag="ssp")
    stream_matmul_TxF(b, c, ONT, Wo, s_out, D, PS[0:2], tag="WO3")

    GM = b.sb("GM", [1, 8], F32)
    SXS = b.sb("SXS", [1, 2], F32)
    RSXC = b.sb("RSXC", [128, 1], F32)
    SXC = b.sb("SXC", [128, 1], F32)
    SW1C = b.sb("SW1C", [128, 64], F32)
    B1C = b.sb("B1C", [128, 64], F32)
    b.dma("sync", GM[:], gmax8, tag="lg")
    b.dma("sync", SW1C[:], sw1, tag="lsw")
    b.dma("sync", B1C[:], b1, tag="lb1")
    b.v("reduce_max", SXS[0:1, 0:1], GM[:], axis=AX.X, tag="gm")
    b.v("tensor_scalar", SXS[0:1, 0:1], SXS[0:1, 0:1], 1.0 / 127, 1e-8, ALU.mult, ALU.max, tag="gm")
    b.v("reciprocal", SXS[0:1, 1:2], SXS[0:1, 0:1], tag="gm")
    b.v("tensor_copy", c["DSTGB"][0:1, 0:1], SXS[0:1, 0:1], tag="gmc")
    b.v("tensor_copy", c["DSTGB"][0:1, 1:2], SXS[0:1, 1:2], tag="gmc")
    b.mm(c["PSR"][:, 0:2], c["ONESROW"][0:1, :], c["DSTGB"][0:1, 0:2], start=True, stop=True, tag="gmb")
    b.v("tensor_copy", SXC[:], c["PSR"][:, 0:1], tag="gmb")
    b.v("tensor_copy", RSXC[:], c["PSR"][:, 1:2], tag="gmb")
    for dt_ in range(ND):
        sl = big3(XT)[:, dt_, :]
        b.v("tensor_single_scalar", c["SCR0"][:], sl, RSXC[:], ALU.mult, tag="qx")
        b.v("tensor_scalar_add", c["SCR0"][:], c["SCR0"][:], MAGIC_RNE, tag="qx")
        b.v("tensor_scalar_add", c["SCR0"][:], c["SCR0"][:], -MAGIC_RNE, tag="qx")
        b.v("tensor_scalar", sl, c["SCR0"][:], 127.0, -127.0, ALU.min, ALU.max, tag="qx")
    HTT = c["WEC"].ap()[:, 0:T]
    for ot in range(64):
        b.dma("gpsimd", c["WEC"].ap()[:, T:T + ND * 128].rearrange("p (m n) -> p m n", m=ND), qw1A[ot], tag=f"h1w{ot}")
        b.v("tensor_tensor", c["SM0"][:, 0:1], SW1C[:, ot:ot + 1], SXC[:], ALU.mult, tag="h1s")
        for tc in range(4):
            ps = PS[2 + tc % 2]
            for dt_ in range(ND):
                b.mm(ps[:], c["WEC"].ap()[:, T + dt_ * 128:T + (dt_ + 1) * 128], big3(XT)[:, dt_, tc * 512:(tc + 1) * 512],
                     start=(dt_ == 0), stop=(dt_ == ND - 1), tag="h1mm")
            b.s("activation", HTT[:, tc * 512:(tc + 1) * 512], ps[:], AF.Gelu, scale=c["SM0"][:, 0:1], bias=B1C[:, ot:ot + 1], tag="h1g")
        b.v("reduce_max", MXC[:, ot:ot + 1], HTT, axis=AX.X, apply_absolute_value=True, tag="h1mx")
        b.dma("gpsimd", h_out[ot], HTT, tag=f"h1sp{ot}")
    idf = nc.inline_tensor(np.eye(128, dtype=np.float32), "c_idf3")
    ID3 = b.sb("ID3", [128, 128], F32)
    b.dma("sync", ID3[:], idf.ap(), tag="cid3")
    b.v("reduce_max", MXC[:, 0:1], MXC[:], axis=AX.X, tag="hmx")
    b.op("tensor", lambda eng: eng.transpose(c["PSR"][0:1, 0:128], MXC[:, 0:1], ID3[:]),
         [MXC[:], ID3[:]], [c["PSR"][:]], tag="hmtp")
    b.v("reduce_max", c["DSTG"][0:1, 0:1], c["PSR"][0:1, 0:128], axis=AX.X, tag="hmx2")
    b.dma("gpsimd", hm_out, c["DSTG"][0:1, 0:1], tag="hmo")
    b.finalize()
    return nc


def build_layer3c(dbg=False):
    nc = make_nc()
    x_in = nc.dram_tensor("x_in", [T, D], F32, kind="ExternalInput").ap()
    hA = nc.dram_tensor("hA", [NT, 128, 64 * 128], BF16, kind="ExternalInput").ap()
    s3_in = nc.dram_tensor("s3_in", [T, D], BF16, kind="ExternalInput").ap()
    qw2A = nc.dram_tensor("qw2A", [16, 64 * 128, 128], BF16, kind="ExternalInput").ap()
    sw2 = nc.dram_tensor("sw2", [128, 16], F32, kind="ExternalInput").ap()
    b2 = nc.dram_tensor("b2", [128, 16], F32, kind="ExternalInput").ap()
    hmax8 = nc.dram_tensor("hmax8", [1, 8], F32, kind="ExternalInput").ap()
    x_out = nc.dram_tensor("x_out", [T, D], F32, kind="ExternalOutput").ap()
    QH_sp = nc.dram_tensor("QH_sp", [4, 64 * 128, 512], BF16, kind="Internal").ap()
    G3_sp = nc.dram_tensor("G3_sp", [T, D], BF16, kind="Internal").ap()

    b = B(nc)
    c = alloc_common(b, nc, n_big=2, skip=("VH", "QCH", "BROW", "ATT", "KH"))
    QHB = c["BIGA"]
    WB = c["BIGB"]
    PS = c["PS"]
    SXS = b.sb("SXS", [1, 2], F32)
    GM = b.sb("GM", [1, 8], F32)
    RS2C = b.sb("RS2C", [128, 1], F32)
    SX2C = b.sb("SX2C", [128, 1], F32)
    SW2C = b.sb("SW2C", [128, 16], F32)
    B2C = b.sb("B2C", [128, 16], F32)
    b.dma("sync", GM[:], hmax8, tag="lg")
    b.dma("sync", SW2C[:], sw2, tag="lsw")
    b.dma("sync", B2C[:], b2, tag="lb2")
    b.v("reduce_max", SXS[0:1, 0:1], GM[:], axis=AX.X, tag="gm")
    b.v("tensor_scalar", SXS[0:1, 0:1], SXS[0:1, 0:1], 1.0 / 127, 1e-8, ALU.mult, ALU.max, tag="gm")
    b.v("reciprocal", SXS[0:1, 1:2], SXS[0:1, 0:1], tag="gm")
    b.v("tensor_copy", c["DSTGB"][0:1, 0:1], SXS[0:1, 0:1], tag="gmc")
    b.v("tensor_copy", c["DSTGB"][0:1, 1:2], SXS[0:1, 1:2], tag="gmc")
    b.mm(c["PSR"][:, 0:2], c["ONESROW"][0:1, :], c["DSTGB"][0:1, 0:2], start=True, stop=True, tag="gmb")
    b.v("tensor_copy", SX2C[:], c["PSR"][:, 0:1], tag="gmb")
    b.v("tensor_copy", RS2C[:], c["PSR"][:, 1:2], tag="gmb")

    qsp4 = QH_sp.rearrange("a (ot p) t -> a p ot t", p=128)
    for tt in range(NT):
        b.dma("sync", QHB.ap()[:, 0:64 * 128], hA[tt], tag=f"qh{tt}")
        sl = QHB.ap()[:, 0:64 * 128]
        for ch in range(4):
            cs = slice(ch * 2048, (ch + 1) * 2048)
            b.v("tensor_single_scalar", c["SCR0"][:], sl[:, cs], RS2C[:], ALU.mult, tag="q2")
            b.v("tensor_scalar_add", c["SCR0"][:], c["SCR0"][:], MAGIC_RNE, tag="q2")
            b.v("tensor_scalar_add", c["SCR0"][:], c["SCR0"][:], -MAGIC_RNE, tag="q2")
            b.v("tensor_scalar", sl[:, cs], c["SCR0"][:], 127.0, -127.0, ALU.min, ALU.max, tag="q2")
        qv = QHB.ap()[:, 0:64 * 128].rearrange("p (ot t) -> p ot t", ot=64)
        for j in range(8):
            b.dma("gpsimd", qsp4[tt // 4][:, 8 * j:8 * (j + 1), (tt % 4) * 128:(tt % 4 + 1) * 128],
                  qv[:, 8 * j:8 * (j + 1), :], tag=f"qsp{tt}_{j}")
    for tc in range(4):
        for j in range(8):
            b.dma("sync", QHB.ap()[:, 0:64 * 512].rearrange("p (ot t) -> p ot t", ot=64)[:, 8 * j:8 * (j + 1), :],
                  qsp4[tc][:, 8 * j:8 * (j + 1), :], tag=f"lq{tc}_{j}")
        for d2t in range(16):
            b.dma("scalar", WB.ap()[:, 0:64 * 128].rearrange("p (ot n) -> p ot n", ot=64),
                  qw2A.rearrange("a (ot p) n -> a p ot n", p=128)[d2t], tag=f"lw{tc}_{d2t}")
            ps = PS[2 + d2t % 2]
            for ot in range(64):
                b.mm(ps[:], WB.ap()[:, ot * 128:(ot + 1) * 128], QHB.ap()[:, ot * 512:(ot + 1) * 512],
                     start=(ot == 0), stop=(ot == 63), tag="l2mm")
            b.v("tensor_tensor", c["SM0"][:, 0:1], SW2C[:, d2t:d2t + 1], SX2C[:], ALU.mult, tag="l2s")
            b.s("activation", c["TC"][:, 0:512], ps[:], AF.Silu, scale=c["SM0"][:, 0:1], bias=B2C[:, d2t:d2t + 1], tag="l2g")
            stg = c["TD"].ap()[:, 0:512].rearrange("p (m f) -> p m f", m=4)
            b.dma_tp(stg, c["TC"][:, 0:512], tag=f"gtp{tc}_{d2t}")
            b.dma("gpsimd", G3_sp.rearrange("(a m p) e -> p m a e", p=128, m=4)[:, :, tc, d2t * 128:(d2t + 1) * 128],
                  c["TD"].ap()[:, 0:512].rearrange("p (m f) -> p m f", m=4), tag=f"gsp{tc}_{d2t}")
    for tt in range(NT):
        xb = c["SCR0"] if tt % 2 == 0 else c["SCR1"]
        b.dma("scalar", xb[:], x_in[tt * 128:(tt + 1) * 128, :], tag=f"ex{tt}")
        b.dma("sync", c["TA"][:], s3_in[tt * 128:(tt + 1) * 128, :], tag=f"es{tt}")
        b.dma("sync", c["TB"][:], G3_sp[tt * 128:(tt + 1) * 128, :], tag=f"eg{tt}")
        b.v("tensor_tensor", c["TD"][:], c["TA"][:], c["TB"][:], ALU.mult, tag="esg")
        b.v("tensor_add", xb[:], xb[:], c["TD"][:], tag="exx")
        b.dma("gpsimd", x_out[tt * 128:(tt + 1) * 128, :], xb[:], tag=f"sx{tt}")
    b.finalize()
    return nc



def build_prep():
    nc = make_nc()
    inp = {}
    out = {}

    def I(name, shape, dtype=F32):
        inp[name] = nc.dram_tensor(name, list(shape), dtype, kind="ExternalInput").ap()
        return inp[name]

    def O(name, shape, dtype=BF16):
        out[name] = nc.dram_tensor(name, list(shape), dtype, kind="ExternalOutput").ap()
        return out[name]

    I("wq1", [2, 128, 2048]); I("wk1", [2, 128, 2048]); I("wv1", [2, 128, 2048]); I("wo1", [2, 128, 2048])
    I("in2", [2, 128, 2048]); I("gammacol", [128, 1])
    I("gupP", [6, 128, 256], I32); I("gupScT", [6, 128, 16], F16)
    I("upP", [6, 128, 256], I32); I("upScT", [6, 128, 16], F16)
    I("dwnP", [2, 128, 704], I32); I("dwnScT", [2, 128, 44], F16)
    I("win", [9, 128, 2048]); I("wh2i", [4096, 16]); I("wout", [2, 128, 4096])
    I("tg", [6, 128, 2048]); I("tgs", [128, 6]); I("tu", [6, 128, 2048]); I("tus", [128, 6])
    I("td", [2, 128, 5632]); I("tds", [128, 2])
    I("w1s", [8, 128, 2048]); I("w2s", [2, 128, 8192])

    O("wq1T_s", [2048, 256]); O("wk1T_s", [2048, 256]); O("wv1T_s", [2048, 256]); O("wo1T_s", [2048, 256])
    O("q2_s", [2, 128, 2048], FP8)
    O("g0T_s", [2048, 768]); O("u0T_s", [2048, 768]); O("d0T_s", [5632, 256])
    O("winT_s", [2048, 1152]); O("wh2iT", [16, 4096]); O("woutT_s", [4096, 256])
    O("tgT_s", [2048, 768]); O("tuT_s", [2048, 768]); O("tdT_s", [5632, 256])
    O("qw1T_s", [2048, 1024]); O("sw1_s", [128, 8], F32)
    O("qw2T_s", [8192, 256]); O("sw2_s", [128, 2], F32)

    b = B(nc)
    SCRF = b.sb("pSCRF", [128, 2048], F32)
    SCRG = b.sb("pSCRG", [128, 2048], F32)
    TBF = b.sb("pTBF", [128, 2048], BF16)
    STG = b.sb("pSTG", [128, 2048], BF16)
    GCOL = b.sb("pGCOL", [128, 1], F32)
    b.dma("sync", GCOL[:], inp["gammacol"], tag="pg")
    PKD = b.sb("pPKD", [128, 704], I32)
    NIB = b.sb("pNIB", [128, 704], I32)
    NF = b.sb("pNF", [128, 704], F32)
    GE8 = b.sb("pGE8", [128, 704], F32)
    MM_ = b.sb("pMM", [128, 704], F32)
    R1 = b.sb("pR1", [128, 704], F32)
    SGN = b.sb("pSGN", [128, 704], F32)
    SCT = b.sb("pSCT", [128, 44], F16)
    SCF = b.sb("pSCF", [128, 44], F32)
    SCE = b.sb("pSCE", [128, 704], F32)
    DEQ = b.sb("pDEQ", [128, 5632], BF16)
    MXP = b.sb("pMXP", [128, 8], F32)
    RSW = b.sb("pRSW", [128, 1], F32)
    idp = nc.inline_tensor(np.eye(128, dtype=BF), "c_idp")
    IDP = b.sb("pIDP", [128, 128], BF16)
    b.dma("sync", IDP[:], idp.ap(), tag="pid")
    PT = b.ps("ppt", [128, 128], BF16)
    stg3 = STG.ap().rearrange("p (m f) -> p m f", m=ND)

    def tp_store(src_bf_ap, dst_dram, dst_row0, dst_col0, dst_w, ncols=2048):
        nm = ncols // 128
        b.dma_tp(stg3[:, 0:nm, :], src_bf_ap, tag="ptp")
        dr = dst_dram.rearrange("(r p) c -> p r c", p=128)
        b.dma("gpsimd", dr[:, dst_row0 // 128:dst_row0 // 128 + nm, dst_col0:dst_col0 + 128],
              stg3[:, 0:nm, :], tag="pst")

    def transpose_job(src, nb, dst, fold_col=None, ncols=2048):
        for bi in range(nb):
            for c0 in range(0, ncols, 2048):
                w = min(2048, ncols - c0)
                if fold_col is not None:
                    b.dma("sync", SCRF[:, 0:w], src[bi][:, c0:c0 + w], tag="ptl")
                    b.v("tensor_single_scalar", TBF[:, 0:w], SCRF[:, 0:w], fold_col(bi), ALU.mult, tag="ptf")
                else:
                    b.dma("gpsimd", TBF[:, 0:w], src[bi][:, c0:c0 + w], tag="ptl")
                tp_store(TBF[:, 0:w], dst, c0, bi * 128, None, ncols=w)

    transpose_job(inp["wq1"], 2, out["wq1T_s"], fold_col=lambda bi: GCOL[:])
    transpose_job(inp["wk1"], 2, out["wk1T_s"], fold_col=lambda bi: GCOL[:])
    transpose_job(inp["wv1"], 2, out["wv1T_s"])
    transpose_job(inp["wo1"], 2, out["wo1T_s"])
    transpose_job(inp["win"], 9, out["winT_s"])
    transpose_job(inp["wout"], 2, out["woutT_s"], ncols=4096)

    TGS = b.sb("pTGS", [128, 6], F32)
    b.dma("sync", TGS[:], inp["tgs"], tag="ptgs")
    TUS = b.sb("pTUS", [128, 6], F32)
    b.dma("sync", TUS[:], inp["tus"], tag="ptus")
    TDS = b.sb("pTDS", [128, 2], F32)
    b.dma("sync", TDS[:], inp["tds"], tag="ptds")
    transpose_job(inp["tg"], 6, out["tgT_s"], fold_col=lambda bi: TGS[:, bi:bi + 1])
    transpose_job(inp["tu"], 6, out["tuT_s"], fold_col=lambda bi: TUS[:, bi:bi + 1])
    transpose_job(inp["td"], 2, out["tdT_s"], fold_col=lambda bi: TDS[:, bi:bi + 1], ncols=5632)

    Q2T = b.sb("pQ2T", [128, 2048], FP8)
    for bi in range(2):
        b.dma("sync", SCRF[:], inp["in2"][bi], tag="pq2l")
        b.v("tensor_scalar_mul", Q2T[:], SCRF[:], 0.5, tag="pq2")
        b.dma("gpsimd", out["q2_s"][bi], Q2T[:], tag="pq2s")

    for bi in range(32):
        b.dma("gpsimd", TBF[:, 0:16], inp["wh2i"][bi * 128:(bi + 1) * 128, :], tag="pwh")
        b.op("tensor", lambda eng, bi=bi: eng.transpose(PT[0:16, :], TBF[:, 0:16], IDP[:]),
             [TBF[:], IDP[:]], [PT[:]], tag="pwht")
        b.v("tensor_copy", STG[0:16, 0:128], PT[0:16, :], tag="pwhc")
        b.dma("gpsimd", out["wh2iT"][:, bi * 128:(bi + 1) * 128], STG[0:16, 0:128], tag="pwhs")

    def dequant_job(pk, scT, nb, W, ngrp, dst):
        for bi in range(nb):
            b.dma("sync", PKD[:, 0:W], pk[bi], tag="pdl")
            b.dma("sync", SCT[:, 0:ngrp], scT[bi], tag="pdsl")
            b.v("tensor_copy", SCF[:, 0:ngrp], SCT[:, 0:ngrp], tag="pdsc")
            sce_src = bass.AP(SCF.ap().tensor, SCF.ap().offset, [SCF.ap().ap[0], [1, ngrp], [0, 16]])
            b.v("tensor_copy", SCE[:, 0:W].rearrange("p (g i) -> p g i", g=ngrp), sce_src, tag="pdse")
            for k in range(8):
                b.v("tensor_scalar", NIB[:, 0:W], PKD[:, 0:W], 4 * k, 15, ALU.logical_shift_right, ALU.bitwise_and, tag="pdk")
                b.v("tensor_copy", NF[:, 0:W], NIB[:, 0:W], tag="pdk")
                b.v("tensor_single_scalar", GE8[:, 0:W], NF[:, 0:W], 8.0, ALU.is_ge, tag="pdk")
                b.v("scalar_tensor_tensor", MM_[:, 0:W], GE8[:, 0:W], -8.0, NF[:, 0:W], ALU.mult, ALU.add, tag="pdk")
                b.v("tensor_scalar", R1[:, 0:W], MM_[:, 0:W], -4.0, 0.0, ALU.add, ALU.max, tag="pdk")
                b.v("tensor_scalar", NF[:, 0:W], MM_[:, 0:W], -6.0, 0.0, ALU.add, ALU.max, tag="pdk")
                b.v("scalar_tensor_tensor", R1[:, 0:W], R1[:, 0:W], 0.5, NF[:, 0:W], ALU.mult, ALU.add, tag="pdk")
                b.v("scalar_tensor_tensor", R1[:, 0:W], MM_[:, 0:W], 0.5, R1[:, 0:W], ALU.mult, ALU.add, tag="pdk")
                b.v("tensor_scalar", SGN[:, 0:W], GE8[:, 0:W], -2.0, 1.0, ALU.mult, ALU.add, tag="pdk")
                b.v("tensor_mul", R1[:, 0:W], R1[:, 0:W], SGN[:, 0:W], tag="pdk")
                b.v("tensor_tensor", DEQ[:, k * W:(k + 1) * W], R1[:, 0:W], SCE[:, 0:W], ALU.mult, tag="pdk")
            for cc in range((8 * W) // 2048):
                tp_store(DEQ[:, cc * 2048:(cc + 1) * 2048], dst, cc * 2048, bi * 128, None)
            if (8 * W) % 2048:
                rem = (8 * W) % 2048
                nm = rem // 128
                cc0 = ((8 * W) // 2048) * 2048
                b.dma_tp(stg3[:, 0:nm, :], DEQ[:, cc0:cc0 + rem], tag="ptp2")
                dr = dst.rearrange("(r p) c -> p r c", p=128)
                b.dma("gpsimd", dr[:, cc0 // 128:cc0 // 128 + nm, bi * 128:(bi + 1) * 128], stg3[:, 0:nm, :], tag="pst2")

    dequant_job(inp["gupP"], inp["gupScT"], 6, 256, 16, out["g0T_s"])
    dequant_job(inp["upP"], inp["upScT"], 6, 256, 16, out["u0T_s"])
    dequant_job(inp["dwnP"], inp["dwnScT"], 2, 704, 44, out["d0T_s"])

    def quant_job(src, nb, rowlen, dst, sw_out):
        nch = rowlen // 2048
        for bi in range(nb):
            for cc in range(nch):
                b.dma("sync", SCRF[:] if cc == 0 else SCRG[:], src[bi][:, cc * 2048:(cc + 1) * 2048], tag="pql")
                b.v("reduce_max", MXP[:, cc:cc + 1], SCRF[:] if cc == 0 else SCRG[:], axis=AX.X, apply_absolute_value=True, tag="pqm")
            b.v("reduce_max", MXP[:, 7:8], MXP[:, 0:nch] if nch > 1 else MXP[:, 0:1], axis=AX.X, tag="pqm2")
            b.v("tensor_scalar", MXP[:, 7:8], MXP[:, 7:8], 1.0 / 127, 1e-8, ALU.mult, ALU.max, tag="pqm3")
            b.v("reciprocal", RSW[:], MXP[:, 7:8], tag="pqr")
            b.dma("gpsimd", sw_out[:, bi:bi + 1], MXP[:, 7:8], tag="pqsw", allow_slow_non_contiguous=True)
            for cc in range(nch):
                srcT = SCRF if cc == 0 else SCRG
                if cc > 0:
                    b.dma("sync", SCRG[:], src[bi][:, cc * 2048:(cc + 1) * 2048], tag="pql2")
                b.v("tensor_single_scalar", srcT[:], srcT[:], RSW[:], ALU.mult, tag="pqq")
                b.v("tensor_scalar_add", srcT[:], srcT[:], MAGIC_RNE, tag="pqq")
                b.v("tensor_scalar_add", srcT[:], srcT[:], -MAGIC_RNE, tag="pqq")
                b.v("tensor_scalar", TBF[:], srcT[:], 127.0, -127.0, ALU.min, ALU.max, tag="pqq")
                tp_store(TBF[:], dst, cc * 2048, bi * 128, None)

    quant_job(inp["w1s"], 8, 2048, out["qw1T_s"], out["sw1_s"])
    quant_job(inp["w2s"], 2, 8192, out["qw2T_s"], out["sw2_s"])

    b.finalize()
    return nc



_PROGS = {}


def _prog(name, fn):
    if name not in _PROGS:
        _PROGS[name] = fn()
    return _PROGS[name]


IPERM = np.array([8 * (i % 704) + i // 704 for i in range(5632)])


def _prep_in_maps(inputs):
    f32 = np.float32
    Wq1 = inputs["L1_attn_Wq"]; Wk1 = inputs["L1_attn_Wk"]
    Wv1 = inputs["L1_attn_Wv"]; Wo1 = inputs["L1_attn_Wo"]
    gamma = inputs["L1_attn_gamma"]
    in2 = inputs["L1_ffn_input2"]
    gup = inputs["L0_ffn_gate_up_packed"]; gus = inputs["L0_ffn_gate_up_scales"]
    dwn = inputs["L0_ffn_down_packed"]; dws = inputs["L0_ffn_down_scales"]
    Win = inputs["L2_attn_W_in"]; Wh2i = inputs["L2_attn_W_h2i"]; Wout = inputs["L2_attn_W_out"]
    tg = inputs["L2_ffn_gate_w"]; tu = inputs["L2_ffn_up_w"]; td = inputs["L2_ffn_down_w"]
    tgs = inputs["L2_ffn_gate_s"]; tus = inputs["L2_ffn_up_s"]; tds = inputs["L2_ffn_down_s"]
    w1 = inputs["L3_ffn_w1"]; w2 = inputs["L3_ffn_w2"]

    def blocks(a, nb_tot, pad_to):
        out = np.zeros((pad_to * 128, a.shape[1]), a.dtype)
        out[:a.shape[0]] = a
        return out.reshape(pad_to, 128, a.shape[1])

    gP = gup[0:5632][IPERM]
    uP = gup[5632:11264][IPERM]
    gS = np.ascontiguousarray(gus[:, 0:5632][:, IPERM].T)
    uS = np.ascontiguousarray(gus[:, 5632:11264][:, IPERM].T)
    gPb = blocks(gP, 44, 48); uPb = blocks(uP, 44, 48)
    gSb = blocks(gS, 44, 48); uSb = blocks(uS, 44, 48)
    dPb = blocks(dwn, 16, 16)
    dSb = blocks(np.ascontiguousarray(dws.T), 16, 16)
    winb = blocks(Win.astype(f32), 69, 72)
    woutb = blocks(Wout, 16, 16)
    tgb = blocks(tg, 44, 48); tub = blocks(tu, 44, 48); tdb = blocks(td, 16, 16)
    tgsb = np.zeros((48 * 128,), f32); tgsb[:5632] = tgs
    tusb = np.zeros((48 * 128,), f32); tusb[:5632] = tus
    tdsb = tds
    w1b = blocks(w1, 64, 64); w2b = blocks(w2, 16, 16)
    wq1b = blocks(Wq1, 16, 16); wk1b = blocks(Wk1, 16, 16)
    wv1b = blocks(Wv1, 16, 16); wo1b = blocks(Wo1, 16, 16)
    in2b = blocks(in2, 16, 16)
    gcol = gamma.reshape(128, 1).astype(f32)

    maps = []
    for ci in range(8):
        m = {
            "wq1": wq1b[2 * ci:2 * ci + 2], "wk1": wk1b[2 * ci:2 * ci + 2],
            "wv1": wv1b[2 * ci:2 * ci + 2], "wo1": wo1b[2 * ci:2 * ci + 2],
            "in2": in2b[2 * ci:2 * ci + 2], "gammacol": gcol,
            "gupP": gPb[6 * ci:6 * ci + 6], "gupScT": gSb[6 * ci:6 * ci + 6],
            "upP": uPb[6 * ci:6 * ci + 6], "upScT": uSb[6 * ci:6 * ci + 6],
            "dwnP": dPb[2 * ci:2 * ci + 2], "dwnScT": dSb[2 * ci:2 * ci + 2],
            "win": winb[9 * ci:9 * ci + 9], "wh2i": Wh2i, "wout": woutb[2 * ci:2 * ci + 2],
            "tg": tgb[6 * ci:6 * ci + 6], "tgs": np.ascontiguousarray(tgsb.reshape(48, 128).T[:, 6 * ci:6 * ci + 6]),
            "tu": tub[6 * ci:6 * ci + 6], "tus": np.ascontiguousarray(tusb.reshape(48, 128).T[:, 6 * ci:6 * ci + 6]),
            "td": tdb[2 * ci:2 * ci + 2], "tds": np.ascontiguousarray(tdsb.reshape(16, 128).T[:, 2 * ci:2 * ci + 2]),
            "w1s": w1b[8 * ci:8 * ci + 8], "w2s": w2b[2 * ci:2 * ci + 2],
        }
        maps.append({k: np.ascontiguousarray(v) for k, v in m.items()})
    return maps


def _run_prep(inputs):
    ncp = _prog("prep", build_prep)
    pres = run_bass_kernel_spmd(ncp, _prep_in_maps(inputs), core_ids=list(range(NCORES))).results
    cat1 = lambda k: np.concatenate([pres[i][k] for i in range(8)], axis=1)
    wq1T = cat1("wq1T_s"); wk1T = cat1("wk1T_s"); wv1T = cat1("wv1T_s"); wo1T = cat1("wo1T_s")
    q2 = np.concatenate([pres[i]["q2_s"].reshape(256, 2048) for i in range(8)], axis=0)
    g0T = cat1("g0T_s")[:, 0:5632]
    u0T = cat1("u0T_s")[:, 0:5632]
    d0T = cat1("d0T_s")
    winT = cat1("winT_s")[:, 0:8832]
    wh2iT = pres[0]["wh2iT"]
    woutT = cat1("woutT_s")
    tgT = cat1("tgT_s")[:, 0:5632]
    tuT = cat1("tuT_s")[:, 0:5632]
    tdT = cat1("tdT_s")
    qw1T = cat1("qw1T_s")
    sw1 = cat1("sw1_s")
    qw2T = cat1("qw2T_s")
    sw2 = cat1("sw2_s")

    A = np.ascontiguousarray
    gateA = A(g0T.reshape(16, 128, 44, 128).transpose(2, 1, 0, 3)).reshape(44, 128, 2048)
    upA = A(u0T.reshape(16, 128, 44, 128).transpose(2, 1, 0, 3)).reshape(44, 128, 2048)
    downA = A(d0T.reshape(44, 128, 2048))
    WinA = A(winT.reshape(16, 128, 69, 128).transpose(2, 1, 0, 3)).reshape(69, 128, 2048)
    WoutA = A(woutT.reshape(32, 128, 2048))
    tgA = A(tgT.reshape(16, 128, 44, 128).transpose(2, 1, 0, 3)).reshape(44, 128, 2048)
    tuA = A(tuT.reshape(16, 128, 44, 128).transpose(2, 1, 0, 3)).reshape(44, 128, 2048)
    tdA = A(tdT.reshape(44, 128, 2048))
    qw1A = A(qw1T.reshape(16, 128, 64, 128).transpose(2, 1, 0, 3)).reshape(64, 128, 2048)
    qw2A = A(qw2T.reshape(64, 128, 16, 128).transpose(2, 0, 1, 3)).reshape(16, 8192, 128)
    return {"wq1T": wq1T, "wk1T": wk1T, "wv1T": wv1T, "wo1T": wo1T, "q2": q2,
            "gateA": gateA, "upA": upA, "downA": downA,
            "WinA": WinA, "wh2iT": wh2iT, "WoutA": WoutA,
            "tgA": tgA, "tuA": tuA, "tdA": tdA,
            "qw1A": qw1A, "sw1": sw1, "qw2A": qw2A, "sw2": sw2}


def kernel(**inputs):
    f32 = np.float32
    A = np.ascontiguousarray
    inputs = {k: np.asarray(v) for k, v in inputs.items()}
    x = A(inputs["L0_attn_x"].astype(f32, copy=False))

    p = _run_prep(inputs)
    gateA = p["gateA"]; upA = p["upA"]; downA = p["downA"]
    WinA = p["WinA"]; wh2iT = p["wh2iT"]; WoutA = p["WoutA"]
    tgA = p["tgA"]; tuA = p["tuA"]; tdA = p["tdA"]
    qw1A = p["qw1A"]; qw2A = p["qw2A"]
    cores = list(range(NCORES))

    nc0 = _prog("l0", build_layer0)
    maps = [{"x_in": x[i], "Wqkv": inputs["L0_attn_Wqkv"], "bqkv": inputs["L0_attn_bqkv"].reshape(1, -1),
             "Wo": inputs["L0_attn_Wo"], "bo": inputs["L0_attn_bo"].reshape(1, -1),
             "gateA": gateA, "upA": upA, "downA": downA} for i in range(8)]
    r = run_bass_kernel_spmd(nc0, maps, core_ids=cores).results
    x1 = np.stack([r[i]["x_out"] for i in range(8)])

    nc1 = _prog("l1", build_layer1)
    maps = [{"x_in": x1[i], "WqT": p["wq1T"], "WkT": p["wk1T"], "WvT": p["wv1T"], "WoT": p["wo1T"],
             "beta": inputs["L1_attn_beta"].reshape(1, 128),
             "q2": p["q2"]} for i in range(8)]
    r = run_bass_kernel_spmd(nc1, maps, core_ids=cores).results
    x2 = np.stack([r[i]["x_out"] for i in range(8)])

    nc2 = _prog("l2", build_layer2)
    maps = [{"x_in": x2[i], "WinA": WinA, "Wh2iT": wh2iT, "WoutA": WoutA,
             "tgA": tgA, "tuA": tuA, "tdA": tdA,
             "wconv": A(inputs["L2_attn_w_conv"].reshape(32, 128, 4).astype(f32)),
             "wnorm_col": A(inputs["L2_attn_w_norm"].reshape(16, 128).T.astype(f32)),
             "dtb": inputs["L2_attn_dt_bias"].reshape(16, 1).astype(f32),
             "alog": inputs["L2_attn_A_log"].reshape(16, 1).astype(f32)} for i in range(8)]
    r = run_bass_kernel_spmd(nc2, maps, core_ids=cores).results
    x3 = np.stack([r[i]["x_out"] for i in range(8)])

    nca = _prog("l3a", build_layer3a)
    r = run_bass_kernel_spmd(nca, [{"x_in": x3[i]} for i in range(8)], core_ids=cores).results
    gmax8 = np.concatenate([r[i]["m_out"].reshape(1) for i in range(8)]).reshape(1, 8).astype(f32)

    ncb = _prog("l3b", build_layer3b)
    sw1c = A(p["sw1"].astype(f32))
    b1c = A(inputs["L3_ffn_b1"].reshape(64, 128).T.astype(f32))
    maps = [{"x_in": x3[i], "Wq": inputs["L3_attn_Wq"], "Wk": inputs["L3_attn_Wk"],
             "Wv": inputs["L3_attn_Wv"], "Wo": inputs["L3_attn_Wo"],
             "qw1A": qw1A, "sw1": sw1c, "b1": b1c, "gmax8": gmax8} for i in range(8)]
    rb = run_bass_kernel_spmd(ncb, maps, core_ids=cores).results
    hmax8 = np.concatenate([rb[i]["hm_out"].reshape(1) for i in range(8)]).reshape(1, 8).astype(f32)

    ncc = _prog("l3c", build_layer3c)
    sw2c = A(p["sw2"].astype(f32))
    b2c = A(inputs["L3_ffn_b2"].reshape(16, 128).T.astype(f32))
    maps = []
    for i in range(8):
        hA = A(rb[i]["h_out"].reshape(64, 128, 16, 128).transpose(2, 1, 0, 3)).reshape(16, 128, 8192)
        maps.append({"x_in": x3[i], "hA": hA, "s3_in": rb[i]["s3_out"], "qw2A": qw2A,
                     "sw2": sw2c, "b2": b2c, "hmax8": hmax8})
    r = run_bass_kernel_spmd(ncc, maps, core_ids=cores).results
    x4 = np.stack([r[i]["x_out"] for i in range(8)])
    return x4
